# version 14
# baseline (speedup 1.0000x reference)
import numpy as np
import concourse.bass as bass
import concourse.mybir as mybir
from concourse.bass_utils import run_bass_kernel_spmd

F32 = mybir.dt.float32
BF16 = mybir.dt.bfloat16
AF = mybir.ActivationFunctionType
ALU = mybir.AluOpType
AX = mybir.AxisListType

T = 2048
D = 1024
NCH = 16
NG = 4
NP = 8
RING = 8

OFF_X2 = 16384
OFF_XNB = 24576
OFF_SQ = 28672
OFF_CONST = 30720
OFF_STAT = 30752
OFF_ONES = 30784
OFF_W1 = 31040
OFF_B1 = 31104
OFF_W2 = 31168
OFF_BIN = 31232
OFF_B2B = 31264
OFF_I64 = 31296
OFF_BOUTF = 31424
OFF_BOUTP = 35520
OFF_XNT = 37568
OFF_EXPT = 37568
OFF_STAG = 58048
OFF_LNB = 62144
OFF_RECIP = 64192
OFF_GBF = 53952
OFF_MASKST = 66240
OFF_I128 = 67264
OFF_QT = 70336
OFF_KT = 103104
OFF_VTM = 135872
OFF_WDKV = 185024
OFF_WUK = 189120
OFF_WUV = 193216
OFF_WIN = 197312
OFF_DIAG = 198336
OFF_SSB = 185024
OFF_CTXT = 185024
OFF_WUQ = 201408
OFF_WO = 201408
OFF_WOUT = 217792
OFF_CKVT = 219840
OFF_H = 219840
OFF_OUT = 219840


def build_core(debug=False):
    nc = bass.Bass("TRN2", target_bir_lowering=False)

    x_h = nc.dram_tensor("x", [T, D], F32, kind="ExternalInput").ap()
    wdkv_h = nc.dram_tensor("W_DKV", [D, 256], F32, kind="ExternalInput").ap()
    wuk_h = nc.dram_tensor("W_UK", [256, D], F32, kind="ExternalInput").ap()
    wuv_h = nc.dram_tensor("W_UV", [256, D], F32, kind="ExternalInput").ap()
    wuq_h = nc.dram_tensor("W_UQ", [D, D], F32, kind="ExternalInput").ap()
    wo_h = nc.dram_tensor("W_O", [D, D], F32, kind="ExternalInput").ap()
    win_h = nc.dram_tensor("Win", [D, 64], F32, kind="ExternalInput").ap()
    bin_h = nc.dram_tensor("bin_", [64], F32, kind="ExternalInput").ap()
    w1_h = nc.dram_tensor("W1", [64, 16], F32, kind="ExternalInput").ap()
    b1_h = nc.dram_tensor("b1", [64, 16], F32, kind="ExternalInput").ap()
    w2_h = nc.dram_tensor("W2", [64, 16], F32, kind="ExternalInput").ap()
    b2_h = nc.dram_tensor("b2", [64], F32, kind="ExternalInput").ap()
    wout_h = nc.dram_tensor("Wout", [64, D], F32, kind="ExternalInput").ap()
    bout_h = nc.dram_tensor("bout", [1024], F32, kind="ExternalInput").ap()
    consts_h = nc.dram_tensor("consts", [128, 8], F32, kind="ExternalInput").ap()
    y_h = nc.dram_tensor("y", [T, D], F32, kind="ExternalOutput").ap()
    if debug:
        dbg_exp00 = nc.dram_tensor("dbg_exp00", [128, 4096], BF16, kind="ExternalOutput").ap()
        dbg_xnt_early = nc.dram_tensor("dbg_xnt_early", [128, 16384], BF16, kind="ExternalOutput").ap()
        dbg_xnb = nc.dram_tensor("dbg_xnb", [128, 2048], BF16, kind="ExternalOutput").ap()
        dbg_sh = nc.dram_tensor("dbg_sh", [128, 1024], F32, kind="ExternalOutput").ap()
        dbg_x2 = nc.dram_tensor("dbg_x2", [128, 2048], F32, kind="ExternalOutput").ap()
        dbg_stat = nc.dram_tensor("dbg_stat", [128, 8], F32, kind="ExternalOutput").ap()
        dbg_stag00 = nc.dram_tensor("dbg_stag00", [128, 1024], F32, kind="ExternalOutput").ap()
        dbg_recb00 = nc.dram_tensor("dbg_recb00", [128, 1024], BF16, kind="ExternalOutput").ap()
        dbg_ctxg0 = nc.dram_tensor("dbg_ctxg0", [128, 4096], BF16, kind="ExternalOutput").ap()
        dbg_kT = nc.dram_tensor("dbg_kT", [128, 16384], BF16, kind="ExternalOutput").ap()
        dbg_qT = nc.dram_tensor("dbg_qT", [128, 16384], BF16, kind="ExternalOutput").ap()
        dbg_ctx = nc.dram_tensor("dbg_ctx", [128, 8192], BF16, kind="ExternalOutput").ap()
        dbg_vtm = nc.dram_tensor("dbg_vtm", [128, 24576], BF16, kind="ExternalOutput").ap()
        dbg_xnt = nc.dram_tensor("dbg_xnt", [128, 16384], BF16, kind="ExternalOutput").ap()

    arena = nc.alloc_sbuf_tensor("manual_arena", [128, 211648], mybir.dt.uint8)
    base = nc.lookup_mloc(arena).addr

    def sb(name, shape, dtype, off):
        return nc.alloc_sbuf_tensor_at(name, shape, dtype, offset=base + off - 16384)

    x2 = sb("x2", [128, 2048], F32, OFF_X2)
    x3 = sb("x3", [128, 2048], F32, OFF_OUT)
    xnb = sb("xnb", [128, 2048], BF16, OFF_XNB)
    sq = sb("sq", [128, 1024], BF16, OFF_SQ)
    cons = sb("cons", [128, 8], F32, OFF_CONST)
    stat = sb("stat", [128, 8], F32, OFF_STAT)
    onesc = sb("onesc", [128, 128], BF16, OFF_ONES)
    w1f = sb("w1f", [64, 16], F32, OFF_W1)
    b1f = sb("b1f", [64, 16], F32, OFF_B1)
    w2f = sb("w2f", [64, 16], F32, OFF_W2)
    binf = sb("binf", [64, 1], F32, OFF_BIN)
    b2b = sb("b2b", [64, 1], BF16, OFF_B2B)
    i64 = sb("i64", [64, 64], BF16, OFF_I64)
    boutf = sb("boutf", [1, 1024], F32, OFF_BOUTF)
    boutp = sb("boutp", [1, 1024], BF16, OFF_BOUTP)
    xnT = sb("xnT", [128, 16384], BF16, OFF_XNT)
    expT = sb("expT", [128, 8192], BF16, OFF_EXPT)
    stag = sb("stag", [128, 1024], F32, OFF_STAG)
    lnb = sb("lnb", [128, 512], F32, OFF_LNB)
    recb = sb("recb", [128, 1024], BF16, OFF_RECIP)
    gbf = sb("gbf", [64, 2048], BF16, OFF_GBF)
    maskst = sb("maskst", [128, 512], BF16, OFF_MASKST)
    i128 = sb("i128", [128, 128], BF16, OFF_I128)
    qT = sb("qT", [128, 16384], BF16, OFF_QT)
    kT = sb("kT", [128, 16384], BF16, OFF_KT)
    vtm = sb("vtm", [128, 24576], BF16, OFF_VTM)
    wdkvb = sb("wdkvb", [128, 2048], BF16, OFF_WDKV)
    wukb = sb("wukb", [128, 2048], BF16, OFF_WUK)
    wuvb = sb("wuvb", [128, 2048], BF16, OFF_WUV)
    winb = sb("winb", [128, 512], BF16, OFF_WIN)
    diagb = sb("diagb", [64, 1024], BF16, OFF_DIAG)
    ssb = sb("ssb", [64, 2048], F32, OFF_SSB)
    ctxT2 = sb("ctxT2", [128, 8192], BF16, OFF_CTXT)
    wuqb = sb("wuqb", [128, 8192], BF16, OFF_WUQ)
    wob = sb("wob", [128, 8192], BF16, OFF_WO)
    woutb = sb("woutb", [64, 1024], BF16, OFF_WOUT)
    ckvT = sb("ckvT", [128, 4096], BF16, OFF_CKVT)
    hbuf = sb("hbuf", [64, 4096], BF16, OFF_H)
    outb = sb("outb", [128, 2048], F32, OFF_OUT)

    SH0 = nc.alloc_psum_tensor("SH0", [128, 1024], F32)
    SH1 = nc.alloc_psum_tensor("SH1", [128, 1024], F32)
    SH2 = nc.alloc_psum_tensor("SH2", [128, 1024], F32)
    C0 = nc.alloc_psum_tensor("C0", [128, 512], F32)
    C1 = nc.alloc_psum_tensor("C1", [128, 512], F32)
    WOB = C0
    BB = C1
    SH = [SH0, SH1, SH2]

    sems = {}
    ops = {"tensor": [], "scalar": [], "vector": [], "gpsimd": [], "sync": []}
    cnt = {}
    ev = {}

    ctxs = []
    for name in ["T", "S", "V", "G", "DX", "DT", "DW", "DO"]:
        ctxs.append(nc.semaphore(name))
    semh = {}
    for name, c in zip(["T", "S", "V", "G", "DX", "DT", "DW", "DO"], ctxs):
        semh[name] = c.__enter__()
        cnt[name] = 0

    def emit(eng, fn, sem=None, n=1, event=None, waits=()):
        wmap = {}
        for key in waits:
            if key is None:
                continue
            s, v = ev[key]
            wmap[s] = max(wmap.get(s, 0), v)
        wlist = [(semh[s], v) for s, v in wmap.items()]
        if sem is not None:
            cnt[sem] += n
            if event is not None:
                ev[event] = (sem, cnt[sem])
            semarg = (semh[sem], n)
        else:
            semarg = None

        def closure(e, fn=fn, wlist=wlist, semarg=semarg):
            for sh_, v in wlist:
                e.wait_ge(sh_, v)
            ins = fn(e)
            if semarg is not None:
                ins.then_inc(semarg[0], semarg[1])
        ops[eng].append(closure)

    def E(key):
        return key if key in ev else None

    emit("sync", lambda e: e.dma_start(cons[:, :], consts_h), "DX", 16, ("ld", "cons"))
    emit("sync", lambda e: e.dma_start(w1f[:, :], w1_h), "DX", 16)
    emit("sync", lambda e: e.dma_start(b1f[:, :], b1_h), "DX", 16)
    emit("sync", lambda e: e.dma_start(w2f[:, :], w2_h), "DX", 16)
    emit("sync", lambda e: e.dma_start(binf[:, :], bin_h.rearrange("(p o) -> p o", o=1)), "DX", 16)
    emit("sync", lambda e: e.dma_start(boutf[:, :], bout_h.rearrange("(o f) -> o f", o=1)),
         "DX", 16, ("ld", "smalls"))

    emit("gpsimd", lambda e: e.dma_start(
        wdkvb[:, :].rearrange("p (c l) -> p c l", c=8),
        wdkv_h.rearrange("(c p) l -> p c l", c=8)), "DW", 16, ("wl", "dkv"))
    emit("gpsimd", lambda e: e.dma_start(
        wukb[:, :].rearrange("p (c d) -> p c d", c=2),
        wuk_h.rearrange("(c p) d -> p c d", c=2)), "DW", 16, ("wl", "uk"))
    emit("gpsimd", lambda e: e.dma_start(
        wuvb[:, :].rearrange("p (c d) -> p c d", c=2),
        wuv_h.rearrange("(c p) d -> p c d", c=2)), "DW", 16, ("wl", "uv"))
    emit("gpsimd", lambda e: e.dma_start(
        winb[:, :].rearrange("p (c m) -> p c m", c=8),
        win_h.rearrange("(c p) m -> p c m", c=8)), "DW", 16, ("wl", "win"))
    emit("gpsimd", lambda e: e.dma_start(woutb[:, :], wout_h), "DW", 16, ("wl", "wout"))
    emit("gpsimd", lambda e: e.dma_start(b2b[:, :], b2_h.rearrange("(p o) -> p o", o=1)),
         "DW", 16, ("wl", "b2"))
    emit("gpsimd", lambda e: e.dma_start(
        wuqb[:, :].rearrange("p (c d) -> p c d", c=8),
        wuq_h.rearrange("(c p) d -> p c d", c=8)), "DW", 16, ("wl", "uq"))

    emit("vector", lambda e: e.memset(onesc[:, :], 1.0), "V", 1, ("v", "ones"))
    emit("vector", lambda e: e.memset(vtm[:, :], 0.0), "V", 1)
    emit("vector", lambda e: e.memset(
        bass.AP(vtm, 64, [[24576, 128], [1536, 16], [192, 8], [1, 1]]), 1.0),
        "V", 1, ("v", "vtminit"))
    emit("gpsimd", lambda e: e.affine_select(
        i64[:, :], onesc[0:64, 0:64], [[1, 64]], ALU.is_equal, 0.0,
        base=0, channel_multiplier=-1), "G", 1, ("g", "i64"),
        waits=[("v", "ones")])
    for i in range(16):
        emit("vector", lambda e, i=i: e.tensor_scalar(
            diagb[0:64, 64 * i:64 * i + 64], i64[0:64, :], w2f[:, i:i + 1], None, ALU.mult),
            "V", 1, ("v", "diag", i) if i == 15 else None,
            waits=[("g", "i64"), ("ld", "smalls")])

    def xslot(t):
        return (x2 if t % 4 < 2 else x3)[:, 1024 * (t % 2):1024 * (t % 2) + 1024]
    for t in range(NCH + 1):
        sl = 1024 * (t % 2)
        slp = 1024 * ((t - 1) % 2)
        if t < NCH:
            emit("sync", lambda e, t=t: e.dma_start(
                xslot(t), x_h[128 * t:128 * t + 128, :]),
                "DX", 16, ("xload", t),
                waits=[E(("sqp", t - 4))])
            emit("scalar", lambda e, t=t, sl=sl: e.activation(
                xnb[:, sl:sl + 1024], xslot(t), AF.Copy,
                accum_out=stat[:, 0:1]),
                "S", 1, ("xsum", t),
                waits=[("xload", t), ("ld", "cons"), E(("tr", t - 2)),
                       E(("negm", t - 1))])
        else:
            emit("scalar", lambda e: e.activation(stat[:, 5:6], stat[:, 0:1], AF.Copy),
                 "S", 1)
        if t >= 1:
            emit("scalar", lambda e: e.activation(stat[:, 3:4], stat[:, 2:3], AF.Sqrt,
                                                  bias=cons[:, 2:3], scale=cons[:, 1:2]),
                 "S", 1, ("sqrt", t - 1), waits=[("ld", "cons"), E(("rcp", t - 2))])
        if t < NCH:
            emit("vector", lambda e, t=t: e.tensor_scalar(stat[:, 1:2], stat[:, 0:1],
                                                          cons[:, 0:1], None, ALU.mult),
                 "V", 1, ("negm", t), waits=[("xsum", t), E(("sqp", t - 1))])
        if t >= 1:
            emit("vector", lambda e: e.reciprocal(stat[:, 4:5], stat[:, 3:4]),
                 "V", 1, ("rcp", t - 1), waits=[("sqrt", t - 1)])
        if t < NCH:
            if t == 0:
                emit("vector", lambda e: e.tensor_copy(sq[:, 0:512], xnb[:, 0:512]), "V", 1)
            emit("vector", lambda e, sl=sl: e.tensor_scalar(
                xnb[:, sl:sl + 1024], xnb[:, sl:sl + 1024], stat[:, 1:2], None, ALU.add),
                "V", 1, ("lnxc", t))
        if t >= 1:
            emit("vector", lambda e, slp=slp: e.tensor_scalar(
                xnb[:, slp:slp + 1024], xnb[:, slp:slp + 1024], stat[:, 4:5], None, ALU.mult),
                "V", 1, ("xn", t - 1))
            emit("sync", lambda e, t=t, slp=slp: e.dma_start_transpose(
                bass.AP(xnT, 128 * (t - 1), [[16384, 128], [2048, 8], [1, 128]]),
                bass.AP(xnb, slp, [[2048, 128], [128, 8], [1, 128]])),
                "DT", 16, ("tr", t - 1), waits=[("xn", t - 1)])
        if t < NCH:
            emit("scalar", lambda e, t=t: e.activation(
                sq[:, :], xslot(t), AF.Square, bias=stat[:, 1:2],
                accum_out=stat[:, 2:3]),
                "S", 1, ("sqp", t), waits=[("negm", t)])

    slices = [SH0[:, 0:512], SH0[:, 512:1024], SH1[:, 0:512], SH1[:, 512:1024]]
    slice_ev = [None, None, None, None]
    slot = [0]

    def mm_group(tag, mms, out_slice_idx, waits):
        si = slot[0] % 4
        slot[0] += 1
        psl = slices[si]
        allw = list(waits)
        if slice_ev[si] is not None:
            allw.append(slice_ev[si])
        nmm = len(mms)
        for mi, (lf, rf) in enumerate(mms):
            emit("tensor",
                 lambda e, lf=lf, rf=rf, mi=mi, psl=psl, nmm=nmm: e.matmul(
                     psl[0:128, :], lf(), rf(), start=(mi == 0), stop=(mi == nmm - 1)),
                 "T" if mi == nmm - 1 else None, 1,
                 ("mm", tag) if mi == nmm - 1 else None,
                 waits=allw if mi == 0 else ())
        return si, psl

    def evac(tag, si, psl, out_ap_fn):
        emit("vector", lambda e, psl=psl: e.tensor_copy(out_ap_fn(), psl[0:128, :]),
             "V", 1, ("ev", tag), waits=[("mm", tag)])
        slice_ev[si] = ("ev", tag)

    for g4 in range(NG):
        trw = ("tr", 4 * g4 + 3)
        qo = 512 * g4
        for oc2 in range(2):
            tag = ("ckv", g4, oc2)
            mms = [(lambda ic=ic, oc2=oc2: wdkvb[:, 256 * ic + 128 * oc2:256 * ic + 128 * oc2 + 128],
                    lambda ic=ic, qo=qo: xnT[:, 2048 * ic + qo:2048 * ic + qo + 512])
                   for ic in range(8)]
            si, psl = mm_group(tag, mms, None, [trw, ("wl", "dkv")])
            evac(tag, si, psl,
                 lambda oc2=oc2, qo=qo: ckvT[:, 2048 * oc2 + qo:2048 * oc2 + qo + 512])
        for oc in range(8):
            tag = ("k", g4, oc)
            mms = [(lambda lc=lc, oc=oc: wukb[:, 1024 * lc + 128 * oc:1024 * lc + 128 * oc + 128],
                    lambda lc=lc, qo=qo: ckvT[:, 2048 * lc + qo:2048 * lc + qo + 512])
                   for lc in range(2)]
            si, psl = mm_group(tag, mms, None,
                               [("wl", "uk"), ("ev", ("ckv", g4, 0)), ("ev", ("ckv", g4, 1))])
            evac(tag, si, psl,
                 lambda oc=oc, qo=qo: kT[:, 2048 * oc + qo:2048 * oc + qo + 512])
        for tc in range(4 * g4, 4 * g4 + 4):
            for half in range(2):
                tag = ("v", tc, half)
                mms = [(lambda lc=lc, tc=tc: ckvT[:, 2048 * lc + 128 * tc:2048 * lc + 128 * tc + 128],
                        lambda lc=lc, half=half: wuvb[:, 1024 * lc + 512 * half:1024 * lc + 512 * half + 512])
                       for lc in range(2)]
                si, psl = mm_group(tag, mms, None,
                                   [("wl", "uv"), ("ev", ("ckv", g4, 0)), ("ev", ("ckv", g4, 1))])
                def vout(tc=tc, half=half):
                    return bass.AP(vtm, 1536 * tc + 768 * half,
                                   [[24576, 128], [192, 4], [128, 2], [1, 64]])
                emit("vector", lambda e, psl=psl, vout=vout: e.tensor_copy(
                    vout(), psl[0:128, :].rearrange("p (a r j) -> p a r j", a=4, r=2)),
                    "V", 1, ("ev", tag), waits=[("mm", tag), ("v", "vtminit")])
                slice_ev[si] = ("ev", tag)
        for oc in range(8):
            tag = ("q", g4, oc)
            mms = [(lambda ic=ic, oc=oc: wuqb[:, 1024 * ic + 128 * oc:1024 * ic + 128 * oc + 128],
                    lambda ic=ic, qo=qo: xnT[:, 2048 * ic + qo:2048 * ic + qo + 512])
                   for ic in range(8)]
            si, psl = mm_group(tag, mms, None, [("wl", "uq"), trw])
            evac(tag, si, psl,
                 lambda oc=oc, qo=qo: qT[:, 2048 * oc + qo:2048 * oc + qo + 512])

    qlast = ("mm", ("q", 3, 7))

    emit("gpsimd", lambda e: e.dma_start(
        wob[:, :].rearrange("p (c d) -> p c d", c=8),
        wo_h.rearrange("(c p) d -> p c d", c=8)), "DW", 16, ("wl", "wo"),
        waits=[qlast])

    for g4 in range(NG):
        tag = ("s", g4)
        for ic in range(8):
            emit("tensor", lambda e, ic=ic, g4=g4: e.matmul(
                C0[0:64, :], winb[:, 64 * ic:64 * ic + 64],
                xnT[:, 2048 * ic + 512 * g4:2048 * ic + 512 * g4 + 512],
                start=(ic == 0), stop=(ic == 7)),
                "T" if ic == 7 else None, 1, ("mm", tag) if ic == 7 else None,
                waits=[("wl", "win"), ("tr", 15), E(("sev", g4 - 1))] if ic == 0 else ())
        emit("vector", lambda e, g4=g4: e.tensor_scalar(
            ssb[0:64, 512 * g4:512 * g4 + 512], C0[0:64, :], binf[:, 0:1], None, ALU.add),
            "V", 1, ("sev", g4), waits=[("mm", tag), ("ld", "smalls")])
    gbank = [C0, C1, SH2[:, 0:512], SH2[:, 512:1024]]
    for i in range(16):
        hsl = 2048 * (i % 2)
        emit("scalar", lambda e, i=i, hsl=hsl: e.activation(
            hbuf[0:64, hsl:hsl + 2048], ssb[0:64, :], AF.Relu,
            bias=b1f[:, i:i + 1], scale=w1f[:, i:i + 1]),
            "S", 1, ("h", i),
            waits=[("sev", 3), E(("gmm", i - 2, 3)), ("ev", ("v", 15, 1))])
        for g4 in range(NG):
            emit("tensor", lambda e, i=i, g4=g4, hsl=hsl: e.matmul(
                gbank[g4][0:64, :], diagb[0:64, 64 * i:64 * i + 64],
                hbuf[0:64, hsl + 512 * g4:hsl + 512 * g4 + 512],
                start=(i == 0), stop=(i == 15), skip_group_check=True),
                "T", 1, ("gmm", i, g4),
                waits=[("h", i), ("v", "diag", 15)] if g4 == 0 else ())
    for g4 in range(NG):
        emit("vector", lambda e, g4=g4: e.tensor_copy(
            gbf[0:64, 512 * g4:512 * g4 + 512], gbank[g4][0:64, :]),
            "V", 1, ("gev", g4), waits=[("gmm", 15, g4)])
    for h5 in range(2):
        emit("tensor", lambda e, h5=h5: e.matmul(
            SH0[0:1, 0:512], b2b[0:64, 0:1], woutb[0:64, 512 * h5:512 * h5 + 512],
            start=True, stop=True),
            "T", 1, ("b2f", h5),
            waits=[("wl", "wout"), ("wl", "b2"), ("gev", 3), E(("bp", h5 - 1))] +
                  ([sev for sev in [slice_ev[0]] if sev is not None]))
        emit("vector", lambda e, h5=h5: e.tensor_tensor(
            boutp[0:1, 512 * h5:512 * h5 + 512], boutf[0:1, 512 * h5:512 * h5 + 512],
            SH0[0:1, 0:512], ALU.add),
            "V", 1, ("bp", h5), waits=[("b2f", h5), ("ld", "smalls")])

    if debug:
        emit("sync", lambda e: e.dma_start(dbg_xnt_early, xnT[:, :]), "DO", 16,
             ("xntdump",), waits=[("mm", ("s", 3)), ("tr", 15)])
        emit("sync", lambda e: e.dma_start(dbg_xnb, xnb[:, :]), "DO", 16, ("xnbdump",),
             waits=[("xn", 15)])
        emit("sync", lambda e: e.dma_start(dbg_x2, x2[:, :]), "DO", 16, ("x2dump",),
             waits=[("xn", 15)])
        emit("sync", lambda e: e.dma_start(dbg_stat, stat[:, :]), "DO", 16, ("statdump",),
             waits=[("xn", 15)])

    emit("vector", lambda e: e.memset(maskst[:, :], 0.0), "V", 1, ("v", "mz"),
         waits=[("mm", ("s", 3))])
    emit("gpsimd", lambda e: e.affine_select(
        i128[:, :], onesc[:, :], [[1, 128]], ALU.is_equal, 0.0,
        base=0, channel_multiplier=-1), "G", 1, waits=[("mm", ("s", 3)), ("v", "ones")])
    emit("gpsimd", lambda e: e.affine_select(
        maskst[:, 0:128], maskst[:, 0:128], [[1, 128]], ALU.is_ge, -1e30,
        base=0, channel_multiplier=-1), "G", 1, ("g", "maskinit"), waits=[("v", "mz")])

    PV = ("bp", 1)
    chunk_seq = []
    for g in range(NG):
        for p in range(NP):
            for c in range(4 * g + 4):
                chunk_seq.append((g, p, c))
    seq_of = {k: i for i, k in enumerate(chunk_seq)}

    pair_list = [(g, p) for g in range(NG) for p in range(NP)]

    def emit_pair_attention_rest(g, p):
        nch_pair = 4 * g + 4
        for c in range(2, nch_pair + 4, 2):
            if c < nch_pair:
                emit_scores(g, p, c)
                emit_scores(g, p, c + 1)
            if c >= 4:
                emit_ctx(g, p, c - 4)
                emit_ctx(g, p, c - 3)

    def emit_scores(g, p, c):
            seq = seq_of[(g, p, c)]
            j = c - 4 * g
            st = 128 * j if j > 0 else 0
            sh = SH[seq % 3]
            r = seq % RING
            diag = (j >= 0)
            w0 = [E(("exp", seq - 3))]
            if seq == 0:
                w0 += [PV, ("gev", 3), ("sev", 3), ("g", "maskinit"), E(("xntdump",))]
            emit("tensor", lambda e, sh=sh, p=p, c=c, g=g, st=st, diag=diag: e.matmul(
                sh[:, st:512], kT[0:64, 2048 * p + 128 * c:2048 * p + 128 * c + 128],
                qT[0:64, 2048 * p + 512 * g + st:2048 * p + 512 * g + 512],
                start=True, stop=not diag, tile_position=(0, 0), skip_group_check=True),
                None, waits=w0)
            emit("tensor", lambda e, sh=sh, p=p, c=c, g=g, st=st, diag=diag: e.matmul(
                sh[:, 512 + st:1024], kT[64:128, 2048 * p + 128 * c:2048 * p + 128 * c + 128],
                qT[64:128, 2048 * p + 512 * g + st:2048 * p + 512 * g + 512],
                start=True, stop=not diag, tile_position=(64, 0), skip_group_check=True),
                None if diag else "T", 1, None if diag else ("sc", seq))
            if diag:
                emit("tensor", lambda e, sh=sh, st=st: e.matmul(
                    sh[:, st:512], i128[:, :], maskst[:, 0:512 - st],
                    start=False, stop=True, skip_group_check=True), None)
                emit("tensor", lambda e, sh=sh, st=st: e.matmul(
                    sh[:, 512 + st:1024], i128[:, :], maskst[:, 0:512 - st],
                    start=False, stop=True, skip_group_check=True),
                    "T", 1, ("sc", seq))
            if debug and seq == 0:
                emit("vector", lambda e, sh=sh: e.tensor_copy(lnb[:, :], sh[:, 0:512]),
                     "V", 1, waits=[("sc", seq)])
                emit("vector", lambda e, sh=sh: e.tensor_copy(stag[:, 0:512], sh[:, 512:1024]),
                     "V", 1, ("shcap", seq))
                emit("sync", lambda e: e.dma_start(dbg_sh[:, 0:512], lnb[:, :]),
                     "DO", 16, waits=[("shcap", seq)])
                emit("sync", lambda e: e.dma_start(dbg_sh[:, 512:1024], stag[:, 0:512]),
                     "DO", 16, ("shdump",))
            expw = [("sc", seq), E(("ctx", seq - RING))]
            if seq == 0:
                expw.append(("ld", "cons"))
                expw.append(E(("shcap", 0)))
            if seq == RING:
                expw.append(E(("expdump",)))
            if st == 0:
                emit("scalar", lambda e, sh=sh, r=r: e.activation(
                    expT[:, 1024 * r:1024 * r + 1024], sh[:, 0:1024], AF.Exp,
                    scale=0.125),
                    "S", 1, ("exp", seq), waits=expw)
            else:
                emit("scalar", lambda e, sh=sh, r=r, st=st: e.activation(
                    expT[:, 1024 * r + st:1024 * r + 512], sh[:, st:512], AF.Exp,
                    scale=0.125), None, waits=expw)
                emit("scalar", lambda e, sh=sh, r=r, st=st: e.activation(
                    expT[:, 1024 * r + 512 + st:1024 * r + 1024], sh[:, 512 + st:1024],
                    AF.Exp, scale=0.125),
                    "S", 1, ("exp", seq))
            emit_stairs(g, p, c)
    def emit_stairs(g, p, c):
            return

    def emit_ctx(g, p, c):
            seq = seq_of[(g, p, c)]
            j = c - 4 * g
            st = 128 * j if j > 0 else 0
            r = seq % RING
            ctxw = [("exp", seq)]
            if c == 0:
                prev = pair_list.index((g, p)) - 1
                if prev >= 0:
                    ctxw.append(("cev", pair_list[prev]))
                    if prev >= 1:
                        ctxw.append(E(("norm", pair_list[prev - 1])))
                    ctxw.append(last_oev[0])
                else:
                    ctxw.append(PV)
            vb = 1536 * c + 192 * p
            emit("tensor", lambda e, vb=vb, r=r, st=st, c=c, g=g: e.matmul(
                C0[0:65, st:512], vtm[:, vb:vb + 65],
                expT[:, 1024 * r + st:1024 * r + 512],
                start=(c == 0), stop=(c == 4 * g + 3), skip_group_check=True),
                None, waits=ctxw)
            emit("tensor", lambda e, vb=vb, r=r, st=st, c=c, g=g: e.matmul(
                C1[0:128, st:512], vtm[:, vb + 64:vb + 192],
                expT[:, 1024 * r + 512 + st:1024 * r + 1024],
                start=(c == 0), stop=(c == 4 * g + 3), skip_group_check=True),
                "T", 1, ("ctx", seq))

    def emit_pair_evac(g, p):
        pb = 512 * (p % 2)
        pk = (g, p)
        idx = pair_list.index(pk)
        wv = [("ctx", seq_of[(g, p, 4 * g + 3)])]
        if idx >= 2:
            wv.append(("recip", pair_list[idx - 2]))
        if idx == 2:
            wv.append(E(("stagdump",)))
            wv.append(E(("recbdump",)))
        emit("vector", lambda e, pb=pb: e.tensor_copy(stag[64:65, pb:pb + 512], C0[64:65, 0:512]),
             "V", 1, waits=wv + ([E(("shdump",))] if pk == (0, 0) else []))
        emit("vector", lambda e, pb=pb: e.tensor_copy(stag[0:1, pb:pb + 512], C1[0:1, 0:512]),
             "V", 1, ("scp", pk))
        co = 4096 * (g % 2) + 512 * p
        wv2 = []
        if g >= 2:
            wv2.append(("bmm", (4 * (g - 2) + 3, 1)))
        if g == 2 and p == 0:
            wv2.append(E(("ctxg0dump",)))
        emit("vector", lambda e, co=co: e.tensor_copy(ctxT2[0:64, co:co + 512], C0[0:64, 0:512]),
             "V", 1, waits=wv2)
        emit("vector", lambda e, co=co: e.tensor_copy(ctxT2[64:128, co:co + 512], C1[64:128, 0:512]),
             "V", 1, ("cev", pk))

    def emit_pair_tail(g, p):
        pk = (g, p)
        idx = pair_list.index(pk)
        pb = 512 * (p % 2)
        emit("scalar", lambda e, pb=pb: e.activation(lnb[0:65, :], stag[0:65, pb:pb + 512], AF.Ln),
             "S", 1, waits=[("scp", pk), E(("shdump",))])
        wrec = [E(("bc", pair_list[idx - 2]))] if idx >= 2 else []
        emit("scalar", lambda e, pb=pb: e.activation(
            recb[0:65, pb:pb + 512], lnb[0:65, :], AF.Exp, scale=-1.0),
            "S", 1, ("recip", pk), waits=wrec)
        wb = [("recip", pk), ("cev", pk)]
        if idx + 1 < len(pair_list) and ("cev", pair_list[idx + 1]) in ev:
            wb.append(("cev", pair_list[idx + 1]))
        if idx >= 1:
            wb.append(("norm", pair_list[idx - 1]))
        else:
            wb.append(("bp", 1))
        emit("tensor", lambda e, pb=pb: e.matmul(
            BB[0:64, :], onesc[64:65, 0:64], recb[64:65, pb:pb + 512], start=True, stop=True),
            None, waits=wb)
        emit("tensor", lambda e, pb=pb: e.matmul(
            BB[64:128, :], onesc[0:1, 0:64], recb[0:1, pb:pb + 512], start=True, stop=True),
            "T", 1, ("bc", pk))
        co = 4096 * (g % 2) + 512 * p
        emit("vector", lambda e, co=co: e.tensor_tensor(
            ctxT2[:, co:co + 512], ctxT2[:, co:co + 512], BB[0:128, 0:512], ALU.mult),
            "V", 1, ("norm", pk), waits=[("bc", pk), ("cev", pk)])

    last_oev = [None]

    def emit_burst(gm1, bi):
        tc = 4 * gm1 + bi // 2
        half = bi % 2
        gbuf = gm1 % 2
        tl = tc % 4
        if gm1 == 0 and bi == 0:
            emit_x2_load(0)
            emit_x2_load(1)
        wv = [("norm", (gm1, 7)), ("wl", "wo"), last_oev[0], ("gev", 2)]
        wv = [w for w in wv if w is not None]
        for cc in range(8):
            emit("tensor", lambda e, cc=cc, gbuf=gbuf, tl=tl, half=half: e.matmul(
                WOB[0:128, :], ctxT2[:, 4096 * gbuf + 512 * cc + 128 * tl:4096 * gbuf + 512 * cc + 128 * tl + 128],
                wob[:, 1024 * cc + 512 * half:1024 * cc + 512 * half + 512],
                start=(cc == 0), stop=False, skip_group_check=True),
                None, waits=wv if cc == 0 else ())
        emit("tensor", lambda e, tc=tc, half=half: e.matmul(
            WOB[0:128, :], gbf[0:64, 128 * tc:128 * tc + 128],
            woutb[0:64, 512 * half:512 * half + 512],
            start=False, stop=False, skip_group_check=True), None)
        emit("tensor", lambda e, half=half: e.matmul(
            WOB[0:128, :], onesc[0:1, 0:128], boutp[0:1, 512 * half:512 * half + 512],
            start=False, stop=True, skip_group_check=True),
            "T", 1, ("bmm", (tc, half)))
        xoff = 1024 * (tc % 2) + 512 * half
        emit("vector", lambda e, xoff=xoff: e.tensor_tensor(
            outb[:, xoff:xoff + 512], x2[:, xoff:xoff + 512], WOB[0:128, 0:512], ALU.add),
            "V", 1, ("oev", (tc, half)),
            waits=[("bmm", (tc, half)), ("x2", tc), E(("ost", (tc - 2, half)))])
        last_oev[0] = ("oev", (tc, half))
        emit("sync", lambda e, tc=tc, half=half, xoff=xoff: e.dma_start(
            y_h[128 * tc:128 * tc + 128, 512 * half:512 * half + 512],
            outb[:, xoff:xoff + 512]),
            "DO", 16, ("ost", (tc, half)), waits=[("oev", (tc, half))])
        if half == 1 and tc + 2 <= 15:
            emit_x2_load(tc + 2)

    def emit_x2_load(tc):
        wv = [("lnxc", 15), E(("x2dump",))]
        if tc >= 2:
            wv.append(("oev", (tc - 2, 1)))
        emit("sync", lambda e, tc=tc: e.dma_start(
            x2[:, 1024 * (tc % 2):1024 * (tc % 2) + 1024], x_h[128 * tc:128 * tc + 128, :]),
            "DX", 16, ("x2", tc), waits=wv)

    all_bursts = [(gm1, b) for gm1 in range(4) for b in range(8)]
    for idx, (g, p) in enumerate(pair_list):
        emit_scores(g, p, 0)
        emit_scores(g, p, 1)
        if idx >= 2:
            emit_pair_tail(*pair_list[idx - 2])
        bk = idx - 9
        if 0 <= bk < 24:
            emit_burst(*all_bursts[bk])
        emit_pair_attention_rest(g, p)
        emit_pair_evac(g, p)
        if debug and idx == 0:
            emit("sync", lambda e: e.dma_start(dbg_exp00, expT[:, :]), "DO", 16,
                 ("expdump",), waits=[("ctx", seq_of[(0, 0, 3)])])
        if debug and idx == 1:
            emit("sync", lambda e: e.dma_start(dbg_stag00, stag[:, :]), "DO", 16,
                 ("stagdump",), waits=[("scp", (0, 0)), ("recip", (0, 0))])
            emit("sync", lambda e: e.dma_start(dbg_recb00, recb[:, :]), "DO", 16, ("recbdump",))
        if debug and idx == 8:
            emit("sync", lambda e: e.dma_start(dbg_ctxg0, ctxT2[:, 0:4096]), "DO", 16,
                 ("ctxg0dump",), waits=[("norm", (0, 7))])
    emit_pair_tail(3, 6)
    emit_pair_tail(3, 7)
    for bk in range(23, 32):
        emit_burst(*all_bursts[bk])

    if debug:
        emit("sync", lambda e: e.dma_start(dbg_kT, kT[:, :]), "DO", 16,
             waits=[last_oev[0]])
        emit("sync", lambda e: e.dma_start(dbg_qT, qT[:, :]), "DO", 16)
        emit("sync", lambda e: e.dma_start(dbg_ctx, ctxT2[:, :]), "DO", 16)
        emit("sync", lambda e: e.dma_start(dbg_vtm, vtm[:, :]), "DO", 16)
        emit("sync", lambda e: e.dma_start(dbg_xnt, xnT[:, :]), "DO", 16)

    nstores = cnt["DO"]

    with nc.Block() as block:
        @block.sync
        def _(sync):
            for f in ops["sync"]:
                f(sync)
            sync.wait_ge(semh["DO"], nstores)

        @block.tensor
        def _(tensor):
            for f in ops["tensor"]:
                f(tensor)

        @block.scalar
        def _(scalar):
            for f in ops["scalar"]:
                f(scalar)

        @block.vector
        def _(vector):
            for f in ops["vector"]:
                f(vector)

        @block.gpsimd
        def _(gpsimd):
            for f in ops["gpsimd"]:
                f(gpsimd)

    for c in ctxs:
        pass
    return nc


_NC_CACHE = None


def kernel(L0_attn_x, L0_attn_W_DKV, L0_attn_W_UK, L0_attn_W_UV, L0_attn_W_UQ,
           L0_attn_W_O, L0_ffn_Win, L0_ffn_bin_, L0_ffn_W1, L0_ffn_b1, L0_ffn_W2,
           L0_ffn_b2, L0_ffn_Wout, L0_ffn_bout):
    global _NC_CACHE
    x = np.ascontiguousarray(np.asarray(L0_attn_x, dtype=np.float32))
    consts = np.zeros((128, 8), np.float32)
    consts[:, 0] = -1.0 / 1024.0
    consts[:, 1] = 1.0 / 1024.0
    consts[:, 2] = 1e-5
    consts[:, 3] = 0.125
    consts[:, 4] = -1.0
    base = {
        "W_DKV": np.ascontiguousarray(L0_attn_W_DKV, dtype=np.float32),
        "W_UK": np.ascontiguousarray(L0_attn_W_UK, dtype=np.float32),
        "W_UV": np.ascontiguousarray(L0_attn_W_UV, dtype=np.float32),
        "W_UQ": np.ascontiguousarray(L0_attn_W_UQ, dtype=np.float32),
        "W_O": np.ascontiguousarray(L0_attn_W_O, dtype=np.float32),
        "Win": np.ascontiguousarray(L0_ffn_Win, dtype=np.float32),
        "bin_": np.ascontiguousarray(L0_ffn_bin_, dtype=np.float32),
        "W1": np.ascontiguousarray(L0_ffn_W1, dtype=np.float32),
        "b1": np.ascontiguousarray(L0_ffn_b1, dtype=np.float32),
        "W2": np.ascontiguousarray(L0_ffn_W2, dtype=np.float32),
        "b2": np.ascontiguousarray(L0_ffn_b2, dtype=np.float32),
        "Wout": np.ascontiguousarray(L0_ffn_Wout, dtype=np.float32),
        "bout": np.ascontiguousarray(L0_ffn_bout, dtype=np.float32),
        "consts": consts,
    }
    in_maps = []
    for core in range(8):
        m = dict(base)
        m["x"] = np.ascontiguousarray(x[core])
        in_maps.append(m)
    if _NC_CACHE is None:
        _NC_CACHE = build_core()
    res = run_bass_kernel_spmd(_NC_CACHE, in_maps, core_ids=list(range(8)))
    return np.stack([res.results[i]["y"] for i in range(8)], axis=0)


# version 15
# speedup vs baseline: 1.0117x; 1.0117x over previous
import numpy as np
import concourse.bass as bass
import concourse.mybir as mybir
from concourse.bass_utils import run_bass_kernel_spmd

F32 = mybir.dt.float32
BF16 = mybir.dt.bfloat16
AF = mybir.ActivationFunctionType
ALU = mybir.AluOpType
AX = mybir.AxisListType

T = 2048
D = 1024
NCH = 16
NG = 4
NP = 8
RING = 8

OFF_X2 = 16384
OFF_XNB = 24576
OFF_SQ = 28672
OFF_CONST = 30720
OFF_STAT = 30752
OFF_ONES = 30784
OFF_W1 = 31040
OFF_B1 = 31104
OFF_W2 = 31168
OFF_BIN = 31232
OFF_B2B = 31264
OFF_I64 = 31296
OFF_BOUTF = 31424
OFF_BOUTP = 35520
OFF_XNT = 37568
OFF_EXPT = 37568
OFF_STAG = 58048
OFF_LNB = 62144
OFF_RECIP = 64192
OFF_GBF = 53952
OFF_MASKST = 66240
OFF_I128 = 67264
OFF_QT = 70336
OFF_KT = 103104
OFF_VTM = 135872
OFF_WDKV = 185024
OFF_WUK = 189120
OFF_WUV = 193216
OFF_WIN = 197312
OFF_DIAG = 198336
OFF_SSB = 185024
OFF_CTXT = 185024
OFF_WUQ = 201408
OFF_WO = 201408
OFF_WOUT = 217792
OFF_CKVT = 219840
OFF_H = 219840
OFF_OUT = 219840


def build_core(debug=False):
    nc = bass.Bass("TRN2", target_bir_lowering=False)

    x_h = nc.dram_tensor("x", [T, D], F32, kind="ExternalInput").ap()
    wdkv_h = nc.dram_tensor("W_DKV", [D, 256], F32, kind="ExternalInput").ap()
    wuk_h = nc.dram_tensor("W_UK", [256, D], F32, kind="ExternalInput").ap()
    wuv_h = nc.dram_tensor("W_UV", [256, D], F32, kind="ExternalInput").ap()
    wuq_h = nc.dram_tensor("W_UQ", [D, D], F32, kind="ExternalInput").ap()
    wo_h = nc.dram_tensor("W_O", [D, D], F32, kind="ExternalInput").ap()
    win_h = nc.dram_tensor("Win", [D, 64], F32, kind="ExternalInput").ap()
    bin_h = nc.dram_tensor("bin_", [64], F32, kind="ExternalInput").ap()
    w1_h = nc.dram_tensor("W1", [64, 16], F32, kind="ExternalInput").ap()
    b1_h = nc.dram_tensor("b1", [64, 16], F32, kind="ExternalInput").ap()
    w2_h = nc.dram_tensor("W2", [64, 16], F32, kind="ExternalInput").ap()
    b2_h = nc.dram_tensor("b2", [64], F32, kind="ExternalInput").ap()
    wout_h = nc.dram_tensor("Wout", [64, D], F32, kind="ExternalInput").ap()
    bout_h = nc.dram_tensor("bout", [1024], F32, kind="ExternalInput").ap()
    consts_h = nc.dram_tensor("consts", [128, 8], F32, kind="ExternalInput").ap()
    y_h = nc.dram_tensor("y", [T, D], F32, kind="ExternalOutput").ap()
    if debug:
        dbg_exp00 = nc.dram_tensor("dbg_exp00", [128, 4096], BF16, kind="ExternalOutput").ap()
        dbg_xnt_early = nc.dram_tensor("dbg_xnt_early", [128, 16384], BF16, kind="ExternalOutput").ap()
        dbg_xnb = nc.dram_tensor("dbg_xnb", [128, 2048], BF16, kind="ExternalOutput").ap()
        dbg_sh = nc.dram_tensor("dbg_sh", [128, 1024], F32, kind="ExternalOutput").ap()
        dbg_x2 = nc.dram_tensor("dbg_x2", [128, 2048], F32, kind="ExternalOutput").ap()
        dbg_stat = nc.dram_tensor("dbg_stat", [128, 8], F32, kind="ExternalOutput").ap()
        dbg_stag00 = nc.dram_tensor("dbg_stag00", [128, 1024], F32, kind="ExternalOutput").ap()
        dbg_recb00 = nc.dram_tensor("dbg_recb00", [128, 1024], BF16, kind="ExternalOutput").ap()
        dbg_ctxg0 = nc.dram_tensor("dbg_ctxg0", [128, 4096], BF16, kind="ExternalOutput").ap()
        dbg_kT = nc.dram_tensor("dbg_kT", [128, 16384], BF16, kind="ExternalOutput").ap()
        dbg_qT = nc.dram_tensor("dbg_qT", [128, 16384], BF16, kind="ExternalOutput").ap()
        dbg_ctx = nc.dram_tensor("dbg_ctx", [128, 8192], BF16, kind="ExternalOutput").ap()
        dbg_vtm = nc.dram_tensor("dbg_vtm", [128, 24576], BF16, kind="ExternalOutput").ap()
        dbg_xnt = nc.dram_tensor("dbg_xnt", [128, 16384], BF16, kind="ExternalOutput").ap()

    arena = nc.alloc_sbuf_tensor("manual_arena", [128, 211648], mybir.dt.uint8)
    base = nc.lookup_mloc(arena).addr

    def sb(name, shape, dtype, off):
        return nc.alloc_sbuf_tensor_at(name, shape, dtype, offset=base + off - 16384)

    x2 = sb("x2", [128, 2048], F32, OFF_X2)
    x3 = sb("x3", [128, 2048], F32, OFF_OUT)
    xnb = sb("xnb", [128, 2048], BF16, OFF_XNB)
    sq = sb("sq", [128, 1024], BF16, OFF_SQ)
    cons = sb("cons", [128, 8], F32, OFF_CONST)
    stat = sb("stat", [128, 8], F32, OFF_STAT)
    onesc = sb("onesc", [128, 128], BF16, OFF_ONES)
    w1f = sb("w1f", [64, 16], F32, OFF_W1)
    b1f = sb("b1f", [64, 16], F32, OFF_B1)
    w2f = sb("w2f", [64, 16], F32, OFF_W2)
    binf = sb("binf", [64, 1], F32, OFF_BIN)
    b2b = sb("b2b", [64, 1], BF16, OFF_B2B)
    i64 = sb("i64", [64, 64], BF16, OFF_I64)
    boutf = sb("boutf", [1, 1024], F32, OFF_BOUTF)
    boutp = sb("boutp", [1, 1024], BF16, OFF_BOUTP)
    xnT = sb("xnT", [128, 16384], BF16, OFF_XNT)
    expT = sb("expT", [128, 8192], BF16, OFF_EXPT)
    stag = sb("stag", [128, 1024], F32, OFF_STAG)
    lnb = sb("lnb", [128, 512], F32, OFF_LNB)
    recb = sb("recb", [128, 1024], BF16, OFF_RECIP)
    gbf = sb("gbf", [64, 2048], BF16, OFF_GBF)
    maskst = sb("maskst", [128, 512], BF16, OFF_MASKST)
    i128 = sb("i128", [128, 128], BF16, OFF_I128)
    qT = sb("qT", [128, 16384], BF16, OFF_QT)
    kT = sb("kT", [128, 16384], BF16, OFF_KT)
    vtm = sb("vtm", [128, 24576], BF16, OFF_VTM)
    wdkvb = sb("wdkvb", [128, 2048], BF16, OFF_WDKV)
    wukb = sb("wukb", [128, 2048], BF16, OFF_WUK)
    wuvb = sb("wuvb", [128, 2048], BF16, OFF_WUV)
    winb = sb("winb", [128, 512], BF16, OFF_WIN)
    diagb = sb("diagb", [64, 1024], BF16, OFF_DIAG)
    ssb = sb("ssb", [64, 2048], F32, OFF_SSB)
    ctxT2 = sb("ctxT2", [128, 8192], BF16, OFF_CTXT)
    wuqb = sb("wuqb", [128, 8192], BF16, OFF_WUQ)
    wob = sb("wob", [128, 8192], BF16, OFF_WO)
    woutb = sb("woutb", [64, 1024], BF16, OFF_WOUT)
    ckvT = sb("ckvT", [128, 4096], BF16, OFF_CKVT)
    hbuf = sb("hbuf", [64, 4096], BF16, OFF_H)
    outb = sb("outb", [128, 2048], F32, OFF_OUT)

    SH0 = nc.alloc_psum_tensor("SH0", [128, 1024], F32)
    SH1 = nc.alloc_psum_tensor("SH1", [128, 1024], F32)
    SH2 = nc.alloc_psum_tensor("SH2", [128, 1024], F32)
    C0 = nc.alloc_psum_tensor("C0", [128, 512], F32)
    C1 = nc.alloc_psum_tensor("C1", [128, 512], F32)
    WOB = C0
    BB = C1
    SH = [SH0, SH1, SH2]

    sems = {}
    ops = {"tensor": [], "scalar": [], "vector": [], "gpsimd": [], "sync": []}
    cnt = {}
    ev = {}

    ctxs = []
    for name in ["T", "S", "V", "G", "DX", "DT", "DW", "DO"]:
        ctxs.append(nc.semaphore(name))
    semh = {}
    for name, c in zip(["T", "S", "V", "G", "DX", "DT", "DW", "DO"], ctxs):
        semh[name] = c.__enter__()
        cnt[name] = 0

    def emit(eng, fn, sem=None, n=1, event=None, waits=()):
        wmap = {}
        for key in waits:
            if key is None:
                continue
            s, v = ev[key]
            wmap[s] = max(wmap.get(s, 0), v)
        wlist = [(semh[s], v) for s, v in wmap.items()]
        if sem is not None:
            cnt[sem] += n
            if event is not None:
                ev[event] = (sem, cnt[sem])
            semarg = (semh[sem], n)
        else:
            semarg = None

        def closure(e, fn=fn, wlist=wlist, semarg=semarg):
            for sh_, v in wlist:
                e.wait_ge(sh_, v)
            ins = fn(e)
            if semarg is not None:
                ins.then_inc(semarg[0], semarg[1])
        ops[eng].append(closure)

    def E(key):
        return key if key in ev else None

    emit("sync", lambda e: e.dma_start(cons[:, :], consts_h), "DX", 16, ("ld", "cons"))
    emit("sync", lambda e: e.dma_start(w1f[:, :], w1_h), "DX", 16)
    emit("sync", lambda e: e.dma_start(b1f[:, :], b1_h), "DX", 16)
    emit("sync", lambda e: e.dma_start(w2f[:, :], w2_h), "DX", 16)
    emit("sync", lambda e: e.dma_start(binf[:, :], bin_h.rearrange("(p o) -> p o", o=1)), "DX", 16)
    emit("sync", lambda e: e.dma_start(boutf[:, :], bout_h.rearrange("(o f) -> o f", o=1)),
         "DX", 16, ("ld", "smalls"))

    emit("gpsimd", lambda e: e.dma_start(
        wdkvb[:, :].rearrange("p (c l) -> p c l", c=8),
        wdkv_h.rearrange("(c p) l -> p c l", c=8)), "DW", 16, ("wl", "dkv"))
    emit("gpsimd", lambda e: e.dma_start(
        wukb[:, :].rearrange("p (c d) -> p c d", c=2),
        wuk_h.rearrange("(c p) d -> p c d", c=2)), "DW", 16, ("wl", "uk"))
    emit("gpsimd", lambda e: e.dma_start(
        wuvb[:, :].rearrange("p (c d) -> p c d", c=2),
        wuv_h.rearrange("(c p) d -> p c d", c=2)), "DW", 16, ("wl", "uv"))
    emit("gpsimd", lambda e: e.dma_start(
        winb[:, :].rearrange("p (c m) -> p c m", c=8),
        win_h.rearrange("(c p) m -> p c m", c=8)), "DW", 16, ("wl", "win"))
    emit("gpsimd", lambda e: e.dma_start(woutb[:, :], wout_h), "DW", 16, ("wl", "wout"))
    emit("gpsimd", lambda e: e.dma_start(b2b[:, :], b2_h.rearrange("(p o) -> p o", o=1)),
         "DW", 16, ("wl", "b2"))
    emit("gpsimd", lambda e: e.dma_start(
        wuqb[:, :].rearrange("p (c d) -> p c d", c=8),
        wuq_h.rearrange("(c p) d -> p c d", c=8)), "DW", 16, ("wl", "uq"))

    emit("vector", lambda e: e.memset(onesc[:, :], 1.0), "V", 1, ("v", "ones"))
    emit("vector", lambda e: e.memset(vtm[:, :], 0.0), "V", 1)
    emit("vector", lambda e: e.memset(
        bass.AP(vtm, 64, [[24576, 128], [1536, 16], [192, 8], [1, 1]]), 1.0),
        "V", 1, ("v", "vtminit"))
    emit("gpsimd", lambda e: e.affine_select(
        i64[:, :], onesc[0:64, 0:64], [[1, 64]], ALU.is_equal, 0.0,
        base=0, channel_multiplier=-1), "G", 1, ("g", "i64"),
        waits=[("v", "ones")])
    for i in range(16):
        emit("vector", lambda e, i=i: e.tensor_scalar(
            diagb[0:64, 64 * i:64 * i + 64], i64[0:64, :], w2f[:, i:i + 1], None, ALU.mult),
            "V", 1, ("v", "diag", i) if i == 15 else None,
            waits=[("g", "i64"), ("ld", "smalls")])

    def xslot(t):
        return (x2 if t % 4 < 2 else x3)[:, 1024 * (t % 2):1024 * (t % 2) + 1024]
    for t in range(NCH + 1):
        sl = 1024 * (t % 2)
        slp = 1024 * ((t - 1) % 2)
        if t < NCH:
            emit("sync", lambda e, t=t: e.dma_start(
                xslot(t), x_h[128 * t:128 * t + 128, :]),
                "DX", 16, ("xload", t),
                waits=[E(("sqp", t - 4))])
            emit("scalar", lambda e, t=t, sl=sl: e.activation(
                xnb[:, sl:sl + 1024], xslot(t), AF.Copy,
                accum_out=stat[:, 0:1]),
                "S", 1, ("xsum", t),
                waits=[("xload", t), ("ld", "cons"), E(("tr", t - 2)),
                       E(("negm", t - 1))])
        else:
            emit("scalar", lambda e: e.activation(stat[:, 5:6], stat[:, 0:1], AF.Copy),
                 "S", 1)
        if t >= 1:
            emit("scalar", lambda e: e.activation(stat[:, 3:4], stat[:, 2:3], AF.Sqrt,
                                                  bias=cons[:, 2:3], scale=cons[:, 1:2]),
                 "S", 1, ("sqrt", t - 1), waits=[("ld", "cons"), E(("rcp", t - 2))])
        if t < NCH:
            emit("vector", lambda e, t=t: e.tensor_scalar(stat[:, 1:2], stat[:, 0:1],
                                                          cons[:, 0:1], None, ALU.mult),
                 "V", 1, ("negm", t), waits=[("xsum", t), E(("sqp", t - 1))])
        if t >= 1:
            emit("vector", lambda e: e.reciprocal(stat[:, 4:5], stat[:, 3:4]),
                 "V", 1, ("rcp", t - 1), waits=[("sqrt", t - 1)])
        if t < NCH:
            if t == 0:
                emit("vector", lambda e: e.tensor_copy(sq[:, 0:512], xnb[:, 0:512]), "V", 1)
            emit("vector", lambda e, sl=sl: e.tensor_scalar(
                xnb[:, sl:sl + 1024], xnb[:, sl:sl + 1024], stat[:, 1:2], None, ALU.add),
                "V", 1, ("lnxc", t))
        if t >= 1:
            emit("vector", lambda e, slp=slp: e.tensor_scalar(
                xnb[:, slp:slp + 1024], xnb[:, slp:slp + 1024], stat[:, 4:5], None, ALU.mult),
                "V", 1, ("xn", t - 1))
            emit("sync", lambda e, t=t, slp=slp: e.dma_start_transpose(
                bass.AP(xnT, 128 * (t - 1), [[16384, 128], [2048, 8], [1, 128]]),
                bass.AP(xnb, slp, [[2048, 128], [128, 8], [1, 128]])),
                "DT", 16, ("tr", t - 1), waits=[("xn", t - 1)])
        if t < NCH:
            emit("scalar", lambda e, t=t: e.activation(
                sq[:, :], xslot(t), AF.Square, bias=stat[:, 1:2],
                accum_out=stat[:, 2:3]),
                "S", 1, ("sqp", t), waits=[("negm", t)])

    slices = [SH0[:, 0:512], SH0[:, 512:1024], SH1[:, 0:512], SH1[:, 512:1024]]
    slice_ev = [None, None, None, None]
    slot = [0]

    def mm_group(tag, mms, out_slice_idx, waits):
        si = slot[0] % 4
        slot[0] += 1
        psl = slices[si]
        allw = list(waits)
        if slice_ev[si] is not None:
            allw.append(slice_ev[si])
        nmm = len(mms)
        for mi, (lf, rf) in enumerate(mms):
            emit("tensor",
                 lambda e, lf=lf, rf=rf, mi=mi, psl=psl, nmm=nmm: e.matmul(
                     psl[0:128, :], lf(), rf(), start=(mi == 0), stop=(mi == nmm - 1)),
                 "T" if mi == nmm - 1 else None, 1,
                 ("mm", tag) if mi == nmm - 1 else None,
                 waits=allw if mi == 0 else ())
        return si, psl

    def evac(tag, si, psl, out_ap_fn):
        emit("vector", lambda e, psl=psl: e.tensor_copy(out_ap_fn(), psl[0:128, :]),
             "V", 1, ("ev", tag), waits=[("mm", tag)])
        slice_ev[si] = ("ev", tag)

    for g4 in range(NG):
        trw = ("tr", 4 * g4 + 3)
        qo = 512 * g4
        for oc2 in range(2):
            tag = ("ckv", g4, oc2)
            mms = [(lambda ic=ic, oc2=oc2: wdkvb[:, 256 * ic + 128 * oc2:256 * ic + 128 * oc2 + 128],
                    lambda ic=ic, qo=qo: xnT[:, 2048 * ic + qo:2048 * ic + qo + 512])
                   for ic in range(8)]
            si, psl = mm_group(tag, mms, None, [trw, ("wl", "dkv")])
            evac(tag, si, psl,
                 lambda oc2=oc2, qo=qo: ckvT[:, 2048 * oc2 + qo:2048 * oc2 + qo + 512])
        for oc in range(8):
            tag = ("k", g4, oc)
            mms = [(lambda lc=lc, oc=oc: wukb[:, 1024 * lc + 128 * oc:1024 * lc + 128 * oc + 128],
                    lambda lc=lc, qo=qo: ckvT[:, 2048 * lc + qo:2048 * lc + qo + 512])
                   for lc in range(2)]
            si, psl = mm_group(tag, mms, None,
                               [("wl", "uk"), ("ev", ("ckv", g4, 0)), ("ev", ("ckv", g4, 1))])
            evac(tag, si, psl,
                 lambda oc=oc, qo=qo: kT[:, 2048 * oc + qo:2048 * oc + qo + 512])
        for tc in range(4 * g4, 4 * g4 + 4):
            for half in range(2):
                tag = ("v", tc, half)
                mms = [(lambda lc=lc, tc=tc: ckvT[:, 2048 * lc + 128 * tc:2048 * lc + 128 * tc + 128],
                        lambda lc=lc, half=half: wuvb[:, 1024 * lc + 512 * half:1024 * lc + 512 * half + 512])
                       for lc in range(2)]
                si, psl = mm_group(tag, mms, None,
                                   [("wl", "uv"), ("ev", ("ckv", g4, 0)), ("ev", ("ckv", g4, 1))])
                def vout(tc=tc, half=half):
                    return bass.AP(vtm, 1536 * tc + 768 * half,
                                   [[24576, 128], [192, 4], [128, 2], [1, 64]])
                emit("vector", lambda e, psl=psl, vout=vout: e.tensor_copy(
                    vout(), psl[0:128, :].rearrange("p (a r j) -> p a r j", a=4, r=2)),
                    "V", 1, ("ev", tag), waits=[("mm", tag), ("v", "vtminit")])
                slice_ev[si] = ("ev", tag)
        for oc in range(8):
            tag = ("q", g4, oc)
            mms = [(lambda ic=ic, oc=oc: wuqb[:, 1024 * ic + 128 * oc:1024 * ic + 128 * oc + 128],
                    lambda ic=ic, qo=qo: xnT[:, 2048 * ic + qo:2048 * ic + qo + 512])
                   for ic in range(8)]
            si, psl = mm_group(tag, mms, None, [("wl", "uq"), trw])
            evac(tag, si, psl,
                 lambda oc=oc, qo=qo: qT[:, 2048 * oc + qo:2048 * oc + qo + 512])

    qlast = ("mm", ("q", 3, 7))

    emit("gpsimd", lambda e: e.dma_start(
        wob[:, :].rearrange("p (c d) -> p c d", c=8),
        wo_h.rearrange("(c p) d -> p c d", c=8)), "DW", 16, ("wl", "wo"),
        waits=[qlast])

    for g4 in range(NG):
        tag = ("s", g4)
        for ic in range(8):
            emit("tensor", lambda e, ic=ic, g4=g4: e.matmul(
                C0[0:64, :], winb[:, 64 * ic:64 * ic + 64],
                xnT[:, 2048 * ic + 512 * g4:2048 * ic + 512 * g4 + 512],
                start=(ic == 0), stop=(ic == 7)),
                "T" if ic == 7 else None, 1, ("mm", tag) if ic == 7 else None,
                waits=[("wl", "win"), ("tr", 15), E(("sev", g4 - 1))] if ic == 0 else ())
        emit("vector", lambda e, g4=g4: e.tensor_scalar(
            ssb[0:64, 512 * g4:512 * g4 + 512], C0[0:64, :], binf[:, 0:1], None, ALU.add),
            "V", 1, ("sev", g4), waits=[("mm", tag), ("ld", "smalls")])
    gbank = [C0, C1, SH2[:, 0:512], SH2[:, 512:1024]]
    for i in range(16):
        hsl = 2048 * (i % 2)
        emit("scalar", lambda e, i=i, hsl=hsl: e.activation(
            hbuf[0:64, hsl:hsl + 2048], ssb[0:64, :], AF.Relu,
            bias=b1f[:, i:i + 1], scale=w1f[:, i:i + 1]),
            "S", 1, ("h", i),
            waits=[("sev", 3), E(("gmm", i - 2, 3)), ("ev", ("v", 15, 1))])
        for g4 in range(NG):
            emit("tensor", lambda e, i=i, g4=g4, hsl=hsl: e.matmul(
                gbank[g4][0:64, :], diagb[0:64, 64 * i:64 * i + 64],
                hbuf[0:64, hsl + 512 * g4:hsl + 512 * g4 + 512],
                start=(i == 0), stop=(i == 15), skip_group_check=True),
                "T", 1, ("gmm", i, g4),
                waits=[("h", i), ("v", "diag", 15)] if g4 == 0 else ())
    for g4 in range(NG):
        emit("vector", lambda e, g4=g4: e.tensor_copy(
            gbf[0:64, 512 * g4:512 * g4 + 512], gbank[g4][0:64, :]),
            "V", 1, ("gev", g4), waits=[("gmm", 15, g4)])
    for h5 in range(2):
        emit("tensor", lambda e, h5=h5: e.matmul(
            SH0[0:1, 0:512], b2b[0:64, 0:1], woutb[0:64, 512 * h5:512 * h5 + 512],
            start=True, stop=True),
            "T", 1, ("b2f", h5),
            waits=[("wl", "wout"), ("wl", "b2"), ("gev", 3), E(("bp", h5 - 1))] +
                  ([sev for sev in [slice_ev[0]] if sev is not None]))
        emit("vector", lambda e, h5=h5: e.tensor_tensor(
            boutp[0:1, 512 * h5:512 * h5 + 512], boutf[0:1, 512 * h5:512 * h5 + 512],
            SH0[0:1, 0:512], ALU.add),
            "V", 1, ("bp", h5), waits=[("b2f", h5), ("ld", "smalls")])

    if debug:
        emit("sync", lambda e: e.dma_start(dbg_xnt_early, xnT[:, :]), "DO", 16,
             ("xntdump",), waits=[("mm", ("s", 3)), ("tr", 15)])
        emit("sync", lambda e: e.dma_start(dbg_xnb, xnb[:, :]), "DO", 16, ("xnbdump",),
             waits=[("xn", 15)])
        emit("sync", lambda e: e.dma_start(dbg_x2, x2[:, :]), "DO", 16, ("x2dump",),
             waits=[("xn", 15)])
        emit("sync", lambda e: e.dma_start(dbg_stat, stat[:, :]), "DO", 16, ("statdump",),
             waits=[("xn", 15)])

    emit("vector", lambda e: e.memset(maskst[:, :], 0.0), "V", 1, ("v", "mz"),
         waits=[("mm", ("s", 3))])
    emit("gpsimd", lambda e: e.affine_select(
        i128[:, :], onesc[:, :], [[1, 128]], ALU.is_equal, 0.0,
        base=0, channel_multiplier=-1), "G", 1, waits=[("mm", ("s", 3)), ("v", "ones")])
    emit("gpsimd", lambda e: e.affine_select(
        maskst[:, 0:128], maskst[:, 0:128], [[1, 128]], ALU.is_ge, -1e30,
        base=0, channel_multiplier=-1), "G", 1, ("g", "maskinit"), waits=[("v", "mz")])

    PV = ("bp", 1)
    chunk_seq = []
    for g in range(NG):
        for p in range(NP):
            for c in range(4 * g + 4):
                chunk_seq.append((g, p, c))
    seq_of = {k: i for i, k in enumerate(chunk_seq)}

    pair_list = [(g, p) for g in range(NG) for p in range(NP)]

    def emit_pair_attention_rest(g, p):
        nch_pair = 4 * g + 4
        for c in range(2, nch_pair + 4, 2):
            if c < nch_pair:
                emit_scores(g, p, c)
                emit_scores(g, p, c + 1)
            if c >= 4:
                emit_ctx(g, p, c - 4)
                emit_ctx(g, p, c - 3)

    def emit_scores(g, p, c):
            seq = seq_of[(g, p, c)]
            j = c - 4 * g
            st = 128 * j if j > 0 else 0
            sh = SH[seq % 3]
            r = seq % RING
            diag = (j >= 0)
            w0 = [E(("exp", seq - 3))]
            if seq == 0:
                w0 += [PV, ("gev", 3), ("sev", 3), ("g", "maskinit"), E(("xntdump",))]
            emit("tensor", lambda e, sh=sh, p=p, c=c, g=g, st=st, diag=diag: e.matmul(
                sh[:, st:512], kT[0:64, 2048 * p + 128 * c:2048 * p + 128 * c + 128],
                qT[0:64, 2048 * p + 512 * g + st:2048 * p + 512 * g + 512],
                start=True, stop=not diag, tile_position=(0, 0), skip_group_check=True),
                None, waits=w0)
            emit("tensor", lambda e, sh=sh, p=p, c=c, g=g, st=st, diag=diag: e.matmul(
                sh[:, 512 + st:1024], kT[64:128, 2048 * p + 128 * c:2048 * p + 128 * c + 128],
                qT[64:128, 2048 * p + 512 * g + st:2048 * p + 512 * g + 512],
                start=True, stop=not diag, tile_position=(64, 0), skip_group_check=True),
                None if diag else "T", 1, None if diag else ("sc", seq))
            if diag:
                emit("tensor", lambda e, sh=sh, st=st: e.matmul(
                    sh[:, st:st + 128], i128[:, :], maskst[:, 0:128],
                    start=False, stop=True, skip_group_check=True), None)
                emit("tensor", lambda e, sh=sh, st=st: e.matmul(
                    sh[:, 512 + st:512 + st + 128], i128[:, :], maskst[:, 0:128],
                    start=False, stop=True, skip_group_check=True),
                    "T", 1, ("sc", seq))
            if debug and seq == 0:
                emit("vector", lambda e, sh=sh: e.tensor_copy(lnb[:, :], sh[:, 0:512]),
                     "V", 1, waits=[("sc", seq)])
                emit("vector", lambda e, sh=sh: e.tensor_copy(stag[:, 0:512], sh[:, 512:1024]),
                     "V", 1, ("shcap", seq))
                emit("sync", lambda e: e.dma_start(dbg_sh[:, 0:512], lnb[:, :]),
                     "DO", 16, waits=[("shcap", seq)])
                emit("sync", lambda e: e.dma_start(dbg_sh[:, 512:1024], stag[:, 0:512]),
                     "DO", 16, ("shdump",))
            expw = [("sc", seq), E(("ctx", seq - RING))]
            if seq == 0:
                expw.append(("ld", "cons"))
                expw.append(E(("shcap", 0)))
            if seq == RING:
                expw.append(E(("expdump",)))
            if st == 0:
                emit("scalar", lambda e, sh=sh, r=r: e.activation(
                    expT[:, 1024 * r:1024 * r + 1024], sh[:, 0:1024], AF.Exp,
                    scale=0.125),
                    "S", 1, ("exp", seq), waits=expw)
            else:
                emit("scalar", lambda e, sh=sh, r=r, st=st: e.activation(
                    expT[:, 1024 * r + st:1024 * r + 512], sh[:, st:512], AF.Exp,
                    scale=0.125), None, waits=expw)
                emit("scalar", lambda e, sh=sh, r=r, st=st: e.activation(
                    expT[:, 1024 * r + 512 + st:1024 * r + 1024], sh[:, 512 + st:1024],
                    AF.Exp, scale=0.125),
                    "S", 1, ("exp", seq))
            emit_stairs(g, p, c)
    def emit_stairs(g, p, c):
            return

    def emit_ctx(g, p, c):
            seq = seq_of[(g, p, c)]
            j = c - 4 * g
            st = 128 * j if j > 0 else 0
            r = seq % RING
            ctxw = [("exp", seq)]
            if c == 0:
                prev = pair_list.index((g, p)) - 1
                if prev >= 0:
                    ctxw.append(("cev", pair_list[prev]))
                    if prev >= 1:
                        ctxw.append(E(("norm", pair_list[prev - 1])))
                    ctxw.append(last_oev[0])
                else:
                    ctxw.append(PV)
            vb = 1536 * c + 192 * p
            emit("tensor", lambda e, vb=vb, r=r, st=st, c=c, g=g: e.matmul(
                C0[0:65, st:512], vtm[:, vb:vb + 65],
                expT[:, 1024 * r + st:1024 * r + 512],
                start=(c == 0), stop=(c == 4 * g + 3), skip_group_check=True),
                None, waits=ctxw)
            emit("tensor", lambda e, vb=vb, r=r, st=st, c=c, g=g: e.matmul(
                C1[0:128, st:512], vtm[:, vb + 64:vb + 192],
                expT[:, 1024 * r + 512 + st:1024 * r + 1024],
                start=(c == 0), stop=(c == 4 * g + 3), skip_group_check=True),
                "T", 1, ("ctx", seq))

    def emit_pair_evac(g, p):
        pb = 512 * (p % 2)
        pk = (g, p)
        idx = pair_list.index(pk)
        wv = [("ctx", seq_of[(g, p, 4 * g + 3)])]
        if idx >= 2:
            wv.append(("recip", pair_list[idx - 2]))
        if idx == 2:
            wv.append(E(("stagdump",)))
            wv.append(E(("recbdump",)))
        emit("vector", lambda e, pb=pb: e.tensor_copy(stag[64:65, pb:pb + 512], C0[64:65, 0:512]),
             "V", 1, waits=wv + ([E(("shdump",))] if pk == (0, 0) else []))
        emit("vector", lambda e, pb=pb: e.tensor_copy(stag[0:1, pb:pb + 512], C1[0:1, 0:512]),
             "V", 1, ("scp", pk))
        co = 4096 * (g % 2) + 512 * p
        wv2 = []
        if g >= 2:
            wv2.append(("bmm", (4 * (g - 2) + 3, 1)))
        if g == 2 and p == 0:
            wv2.append(E(("ctxg0dump",)))
        emit("vector", lambda e, co=co: e.tensor_copy(ctxT2[0:64, co:co + 512], C0[0:64, 0:512]),
             "V", 1, waits=wv2)
        emit("vector", lambda e, co=co: e.tensor_copy(ctxT2[64:128, co:co + 512], C1[64:128, 0:512]),
             "V", 1, ("cev", pk))

    def emit_pair_tail(g, p):
        pk = (g, p)
        idx = pair_list.index(pk)
        pb = 512 * (p % 2)
        emit("scalar", lambda e, pb=pb: e.activation(lnb[0:65, :], stag[0:65, pb:pb + 512], AF.Ln),
             "S", 1, waits=[("scp", pk), E(("shdump",))])
        wrec = [E(("bc", pair_list[idx - 2]))] if idx >= 2 else []
        emit("scalar", lambda e, pb=pb: e.activation(
            recb[0:65, pb:pb + 512], lnb[0:65, :], AF.Exp, scale=-1.0),
            "S", 1, ("recip", pk), waits=wrec)
        wb = [("recip", pk), ("cev", pk)]
        if idx + 1 < len(pair_list) and ("cev", pair_list[idx + 1]) in ev:
            wb.append(("cev", pair_list[idx + 1]))
        if idx >= 1:
            wb.append(("norm", pair_list[idx - 1]))
        else:
            wb.append(("bp", 1))
        emit("tensor", lambda e, pb=pb: e.matmul(
            BB[0:64, :], onesc[64:65, 0:64], recb[64:65, pb:pb + 512], start=True, stop=True),
            None, waits=wb)
        emit("tensor", lambda e, pb=pb: e.matmul(
            BB[64:128, :], onesc[0:1, 0:64], recb[0:1, pb:pb + 512], start=True, stop=True),
            "T", 1, ("bc", pk))
        co = 4096 * (g % 2) + 512 * p
        emit("vector", lambda e, co=co: e.tensor_tensor(
            ctxT2[:, co:co + 512], ctxT2[:, co:co + 512], BB[0:128, 0:512], ALU.mult),
            "V", 1, ("norm", pk), waits=[("bc", pk), ("cev", pk)])

    last_oev = [None]

    def emit_burst(gm1, bi):
        tc = 4 * gm1 + bi // 2
        half = bi % 2
        gbuf = gm1 % 2
        tl = tc % 4
        if gm1 == 0 and bi == 0:
            emit_x2_load(0)
            emit_x2_load(1)
        wv = [("norm", (gm1, 7)), ("wl", "wo"), last_oev[0], ("gev", 2)]
        wv = [w for w in wv if w is not None]
        for cc in range(8):
            emit("tensor", lambda e, cc=cc, gbuf=gbuf, tl=tl, half=half: e.matmul(
                WOB[0:128, :], ctxT2[:, 4096 * gbuf + 512 * cc + 128 * tl:4096 * gbuf + 512 * cc + 128 * tl + 128],
                wob[:, 1024 * cc + 512 * half:1024 * cc + 512 * half + 512],
                start=(cc == 0), stop=False, skip_group_check=True),
                None, waits=wv if cc == 0 else ())
        emit("tensor", lambda e, tc=tc, half=half: e.matmul(
            WOB[0:128, :], gbf[0:64, 128 * tc:128 * tc + 128],
            woutb[0:64, 512 * half:512 * half + 512],
            start=False, stop=False, skip_group_check=True), None)
        emit("tensor", lambda e, half=half: e.matmul(
            WOB[0:128, :], onesc[0:1, 0:128], boutp[0:1, 512 * half:512 * half + 512],
            start=False, stop=True, skip_group_check=True),
            "T", 1, ("bmm", (tc, half)))
        xoff = 1024 * (tc % 2) + 512 * half
        emit("vector", lambda e, xoff=xoff: e.tensor_tensor(
            outb[:, xoff:xoff + 512], x2[:, xoff:xoff + 512], WOB[0:128, 0:512], ALU.add),
            "V", 1, ("oev", (tc, half)),
            waits=[("bmm", (tc, half)), ("x2", tc), E(("ost", (tc - 2, half)))])
        last_oev[0] = ("oev", (tc, half))
        emit("sync", lambda e, tc=tc, half=half, xoff=xoff: e.dma_start(
            y_h[128 * tc:128 * tc + 128, 512 * half:512 * half + 512],
            outb[:, xoff:xoff + 512]),
            "DO", 16, ("ost", (tc, half)), waits=[("oev", (tc, half))])
        if half == 1 and tc + 2 <= 15:
            emit_x2_load(tc + 2)

    def emit_x2_load(tc):
        wv = [("lnxc", 15), E(("x2dump",))]
        if tc >= 2:
            wv.append(("oev", (tc - 2, 1)))
        emit("sync", lambda e, tc=tc: e.dma_start(
            x2[:, 1024 * (tc % 2):1024 * (tc % 2) + 1024], x_h[128 * tc:128 * tc + 128, :]),
            "DX", 16, ("x2", tc), waits=wv)

    all_bursts = [(gm1, b) for gm1 in range(4) for b in range(8)]
    for idx, (g, p) in enumerate(pair_list):
        emit_scores(g, p, 0)
        emit_scores(g, p, 1)
        if idx >= 2:
            emit_pair_tail(*pair_list[idx - 2])
        bk = idx - 9
        if 0 <= bk < 24:
            emit_burst(*all_bursts[bk])
        emit_pair_attention_rest(g, p)
        emit_pair_evac(g, p)
        if debug and idx == 0:
            emit("sync", lambda e: e.dma_start(dbg_exp00, expT[:, :]), "DO", 16,
                 ("expdump",), waits=[("ctx", seq_of[(0, 0, 3)])])
        if debug and idx == 1:
            emit("sync", lambda e: e.dma_start(dbg_stag00, stag[:, :]), "DO", 16,
                 ("stagdump",), waits=[("scp", (0, 0)), ("recip", (0, 0))])
            emit("sync", lambda e: e.dma_start(dbg_recb00, recb[:, :]), "DO", 16, ("recbdump",))
        if debug and idx == 8:
            emit("sync", lambda e: e.dma_start(dbg_ctxg0, ctxT2[:, 0:4096]), "DO", 16,
                 ("ctxg0dump",), waits=[("norm", (0, 7))])
    emit_pair_tail(3, 6)
    emit_pair_tail(3, 7)
    for bk in range(23, 32):
        emit_burst(*all_bursts[bk])

    if debug:
        emit("sync", lambda e: e.dma_start(dbg_kT, kT[:, :]), "DO", 16,
             waits=[last_oev[0]])
        emit("sync", lambda e: e.dma_start(dbg_qT, qT[:, :]), "DO", 16)
        emit("sync", lambda e: e.dma_start(dbg_ctx, ctxT2[:, :]), "DO", 16)
        emit("sync", lambda e: e.dma_start(dbg_vtm, vtm[:, :]), "DO", 16)
        emit("sync", lambda e: e.dma_start(dbg_xnt, xnT[:, :]), "DO", 16)

    nstores = cnt["DO"]

    with nc.Block() as block:
        @block.sync
        def _(sync):
            for f in ops["sync"]:
                f(sync)
            sync.wait_ge(semh["DO"], nstores)

        @block.tensor
        def _(tensor):
            for f in ops["tensor"]:
                f(tensor)

        @block.scalar
        def _(scalar):
            for f in ops["scalar"]:
                f(scalar)

        @block.vector
        def _(vector):
            for f in ops["vector"]:
                f(vector)

        @block.gpsimd
        def _(gpsimd):
            for f in ops["gpsimd"]:
                f(gpsimd)

    for c in ctxs:
        pass
    return nc


_NC_CACHE = None


def kernel(L0_attn_x, L0_attn_W_DKV, L0_attn_W_UK, L0_attn_W_UV, L0_attn_W_UQ,
           L0_attn_W_O, L0_ffn_Win, L0_ffn_bin_, L0_ffn_W1, L0_ffn_b1, L0_ffn_W2,
           L0_ffn_b2, L0_ffn_Wout, L0_ffn_bout):
    global _NC_CACHE
    x = np.ascontiguousarray(np.asarray(L0_attn_x, dtype=np.float32))
    consts = np.zeros((128, 8), np.float32)
    consts[:, 0] = -1.0 / 1024.0
    consts[:, 1] = 1.0 / 1024.0
    consts[:, 2] = 1e-5
    consts[:, 3] = 0.125
    consts[:, 4] = -1.0
    base = {
        "W_DKV": np.ascontiguousarray(L0_attn_W_DKV, dtype=np.float32),
        "W_UK": np.ascontiguousarray(L0_attn_W_UK, dtype=np.float32),
        "W_UV": np.ascontiguousarray(L0_attn_W_UV, dtype=np.float32),
        "W_UQ": np.ascontiguousarray(L0_attn_W_UQ, dtype=np.float32),
        "W_O": np.ascontiguousarray(L0_attn_W_O, dtype=np.float32),
        "Win": np.ascontiguousarray(L0_ffn_Win, dtype=np.float32),
        "bin_": np.ascontiguousarray(L0_ffn_bin_, dtype=np.float32),
        "W1": np.ascontiguousarray(L0_ffn_W1, dtype=np.float32),
        "b1": np.ascontiguousarray(L0_ffn_b1, dtype=np.float32),
        "W2": np.ascontiguousarray(L0_ffn_W2, dtype=np.float32),
        "b2": np.ascontiguousarray(L0_ffn_b2, dtype=np.float32),
        "Wout": np.ascontiguousarray(L0_ffn_Wout, dtype=np.float32),
        "bout": np.ascontiguousarray(L0_ffn_bout, dtype=np.float32),
        "consts": consts,
    }
    in_maps = []
    for core in range(8):
        m = dict(base)
        m["x"] = np.ascontiguousarray(x[core])
        in_maps.append(m)
    if _NC_CACHE is None:
        _NC_CACHE = build_core()
    res = run_bass_kernel_spmd(_NC_CACHE, in_maps, core_ids=list(range(8)))
    return np.stack([res.results[i]["y"] for i in range(8)], axis=0)


# version 16
# speedup vs baseline: 1.0122x; 1.0005x over previous
import numpy as np
import concourse.bass as bass
import concourse.mybir as mybir
from concourse.bass_utils import run_bass_kernel_spmd

F32 = mybir.dt.float32
BF16 = mybir.dt.bfloat16
AF = mybir.ActivationFunctionType
ALU = mybir.AluOpType
AX = mybir.AxisListType

T = 2048
D = 1024
NCH = 16
NG = 4
NP = 8
RING = 8

OFF_X2 = 16384
OFF_XNB = 24576
OFF_SQ = 28672
OFF_CONST = 30720
OFF_STAT = 30752
OFF_ONES = 30784
OFF_W1 = 31040
OFF_B1 = 31104
OFF_W2 = 31168
OFF_BIN = 31232
OFF_B2B = 31264
OFF_I64 = 31296
OFF_BOUTF = 31424
OFF_BOUTP = 35520
OFF_XNT = 37568
OFF_EXPT = 37568
OFF_STAG = 58048
OFF_LNB = 62144
OFF_RECIP = 64192
OFF_GBF = 53952
OFF_MASKST = 66240
OFF_I128 = 67264
OFF_QT = 70336
OFF_KT = 103104
OFF_VTM = 135872
OFF_WDKV = 185024
OFF_WUK = 189120
OFF_WUV = 193216
OFF_WIN = 197312
OFF_DIAG = 198336
OFF_SSB = 185024
OFF_CTXT = 185024
OFF_WUQ = 201408
OFF_WO = 201408
OFF_WOUT = 217792
OFF_CKVT = 219840
OFF_H = 219840
OFF_OUT = 219840


def build_core(debug=False):
    nc = bass.Bass("TRN2", target_bir_lowering=False, detect_race_conditions=False)

    x_h = nc.dram_tensor("x", [T, D], F32, kind="ExternalInput").ap()
    wdkv_h = nc.dram_tensor("W_DKV", [D, 256], F32, kind="ExternalInput").ap()
    wuk_h = nc.dram_tensor("W_UK", [256, D], F32, kind="ExternalInput").ap()
    wuv_h = nc.dram_tensor("W_UV", [256, D], F32, kind="ExternalInput").ap()
    wuq_h = nc.dram_tensor("W_UQ", [D, D], F32, kind="ExternalInput").ap()
    wo_h = nc.dram_tensor("W_O", [D, D], F32, kind="ExternalInput").ap()
    win_h = nc.dram_tensor("Win", [D, 64], F32, kind="ExternalInput").ap()
    bin_h = nc.dram_tensor("bin_", [64], F32, kind="ExternalInput").ap()
    w1_h = nc.dram_tensor("W1", [64, 16], F32, kind="ExternalInput").ap()
    b1_h = nc.dram_tensor("b1", [64, 16], F32, kind="ExternalInput").ap()
    w2_h = nc.dram_tensor("W2", [64, 16], F32, kind="ExternalInput").ap()
    b2_h = nc.dram_tensor("b2", [64], F32, kind="ExternalInput").ap()
    wout_h = nc.dram_tensor("Wout", [64, D], F32, kind="ExternalInput").ap()
    bout_h = nc.dram_tensor("bout", [1024], F32, kind="ExternalInput").ap()
    consts_h = nc.dram_tensor("consts", [128, 8], F32, kind="ExternalInput").ap()
    y_h = nc.dram_tensor("y", [T, D], F32, kind="ExternalOutput").ap()
    if debug:
        dbg_exp00 = nc.dram_tensor("dbg_exp00", [128, 4096], BF16, kind="ExternalOutput").ap()
        dbg_xnt_early = nc.dram_tensor("dbg_xnt_early", [128, 16384], BF16, kind="ExternalOutput").ap()
        dbg_xnb = nc.dram_tensor("dbg_xnb", [128, 2048], BF16, kind="ExternalOutput").ap()
        dbg_sh = nc.dram_tensor("dbg_sh", [128, 1024], F32, kind="ExternalOutput").ap()
        dbg_x2 = nc.dram_tensor("dbg_x2", [128, 2048], F32, kind="ExternalOutput").ap()
        dbg_stat = nc.dram_tensor("dbg_stat", [128, 8], F32, kind="ExternalOutput").ap()
        dbg_stag00 = nc.dram_tensor("dbg_stag00", [128, 1024], F32, kind="ExternalOutput").ap()
        dbg_recb00 = nc.dram_tensor("dbg_recb00", [128, 1024], BF16, kind="ExternalOutput").ap()
        dbg_ctxg0 = nc.dram_tensor("dbg_ctxg0", [128, 4096], BF16, kind="ExternalOutput").ap()
        dbg_kT = nc.dram_tensor("dbg_kT", [128, 16384], BF16, kind="ExternalOutput").ap()
        dbg_qT = nc.dram_tensor("dbg_qT", [128, 16384], BF16, kind="ExternalOutput").ap()
        dbg_ctx = nc.dram_tensor("dbg_ctx", [128, 8192], BF16, kind="ExternalOutput").ap()
        dbg_vtm = nc.dram_tensor("dbg_vtm", [128, 24576], BF16, kind="ExternalOutput").ap()
        dbg_xnt = nc.dram_tensor("dbg_xnt", [128, 16384], BF16, kind="ExternalOutput").ap()

    arena = nc.alloc_sbuf_tensor("manual_arena", [128, 211648], mybir.dt.uint8)
    base = nc.lookup_mloc(arena).addr

    def sb(name, shape, dtype, off):
        return nc.alloc_sbuf_tensor_at(name, shape, dtype, offset=base + off - 16384)

    x2 = sb("x2", [128, 2048], F32, OFF_X2)
    x3 = sb("x3", [128, 2048], F32, OFF_OUT)
    xnb = sb("xnb", [128, 2048], BF16, OFF_XNB)
    sq = sb("sq", [128, 1024], BF16, OFF_SQ)
    cons = sb("cons", [128, 8], F32, OFF_CONST)
    stat = sb("stat", [128, 8], F32, OFF_STAT)
    onesc = sb("onesc", [128, 128], BF16, OFF_ONES)
    w1f = sb("w1f", [64, 16], F32, OFF_W1)
    b1f = sb("b1f", [64, 16], F32, OFF_B1)
    w2f = sb("w2f", [64, 16], F32, OFF_W2)
    binf = sb("binf", [64, 1], F32, OFF_BIN)
    b2b = sb("b2b", [64, 1], BF16, OFF_B2B)
    i64 = sb("i64", [64, 64], BF16, OFF_I64)
    boutf = sb("boutf", [1, 1024], F32, OFF_BOUTF)
    boutp = sb("boutp", [1, 1024], BF16, OFF_BOUTP)
    xnT = sb("xnT", [128, 16384], BF16, OFF_XNT)
    expT = sb("expT", [128, 8192], BF16, OFF_EXPT)
    stag = sb("stag", [128, 1024], F32, OFF_STAG)
    lnb = sb("lnb", [128, 512], F32, OFF_LNB)
    recb = sb("recb", [128, 1024], BF16, OFF_RECIP)
    gbf = sb("gbf", [64, 2048], BF16, OFF_GBF)
    maskst = sb("maskst", [128, 512], BF16, OFF_MASKST)
    i128 = sb("i128", [128, 128], BF16, OFF_I128)
    qT = sb("qT", [128, 16384], BF16, OFF_QT)
    kT = sb("kT", [128, 16384], BF16, OFF_KT)
    vtm = sb("vtm", [128, 24576], BF16, OFF_VTM)
    wdkvb = sb("wdkvb", [128, 2048], BF16, OFF_WDKV)
    wukb = sb("wukb", [128, 2048], BF16, OFF_WUK)
    wuvb = sb("wuvb", [128, 2048], BF16, OFF_WUV)
    winb = sb("winb", [128, 512], BF16, OFF_WIN)
    diagb = sb("diagb", [64, 1024], BF16, OFF_DIAG)
    ssb = sb("ssb", [64, 2048], F32, OFF_SSB)
    ctxT2 = sb("ctxT2", [128, 8192], BF16, OFF_CTXT)
    wuqb = sb("wuqb", [128, 8192], BF16, OFF_WUQ)
    wob = sb("wob", [128, 8192], BF16, OFF_WO)
    woutb = sb("woutb", [64, 1024], BF16, OFF_WOUT)
    ckvT = sb("ckvT", [128, 4096], BF16, OFF_CKVT)
    hbuf = sb("hbuf", [64, 4096], BF16, OFF_H)
    outb = sb("outb", [128, 2048], F32, OFF_OUT)

    SH0 = nc.alloc_psum_tensor("SH0", [128, 1024], F32)
    SH1 = nc.alloc_psum_tensor("SH1", [128, 1024], F32)
    SH2 = nc.alloc_psum_tensor("SH2", [128, 1024], F32)
    C0 = nc.alloc_psum_tensor("C0", [128, 512], F32)
    C1 = nc.alloc_psum_tensor("C1", [128, 512], F32)
    WOB = C0
    BB = C1
    SH = [SH0, SH1, SH2]

    sems = {}
    ops = {"tensor": [], "scalar": [], "vector": [], "gpsimd": [], "sync": []}
    cnt = {}
    ev = {}

    ctxs = []
    for name in ["T", "S", "V", "G", "DX", "DT", "DW", "DO"]:
        ctxs.append(nc.semaphore(name))
    semh = {}
    for name, c in zip(["T", "S", "V", "G", "DX", "DT", "DW", "DO"], ctxs):
        semh[name] = c.__enter__()
        cnt[name] = 0

    def emit(eng, fn, sem=None, n=1, event=None, waits=()):
        wmap = {}
        for key in waits:
            if key is None:
                continue
            s, v = ev[key]
            wmap[s] = max(wmap.get(s, 0), v)
        wlist = [(semh[s], v) for s, v in wmap.items()]
        if sem is not None:
            cnt[sem] += n
            if event is not None:
                ev[event] = (sem, cnt[sem])
            semarg = (semh[sem], n)
        else:
            semarg = None

        def closure(e, fn=fn, wlist=wlist, semarg=semarg):
            for sh_, v in wlist:
                e.wait_ge(sh_, v)
            ins = fn(e)
            if semarg is not None:
                ins.then_inc(semarg[0], semarg[1])
        ops[eng].append(closure)

    def E(key):
        return key if key in ev else None

    emit("sync", lambda e: e.dma_start(cons[:, :], consts_h), "DX", 16, ("ld", "cons"))
    emit("sync", lambda e: e.dma_start(w1f[:, :], w1_h), "DX", 16)
    emit("sync", lambda e: e.dma_start(b1f[:, :], b1_h), "DX", 16)
    emit("sync", lambda e: e.dma_start(w2f[:, :], w2_h), "DX", 16)
    emit("sync", lambda e: e.dma_start(binf[:, :], bin_h.rearrange("(p o) -> p o", o=1)), "DX", 16)
    emit("sync", lambda e: e.dma_start(boutf[:, :], bout_h.rearrange("(o f) -> o f", o=1)),
         "DX", 16, ("ld", "smalls"))

    emit("gpsimd", lambda e: e.dma_start(
        wdkvb[:, :].rearrange("p (c l) -> p c l", c=8),
        wdkv_h.rearrange("(c p) l -> p c l", c=8)), "DW", 16, ("wl", "dkv"))
    emit("gpsimd", lambda e: e.dma_start(
        wukb[:, :].rearrange("p (c d) -> p c d", c=2),
        wuk_h.rearrange("(c p) d -> p c d", c=2)), "DW", 16, ("wl", "uk"))
    emit("gpsimd", lambda e: e.dma_start(
        wuvb[:, :].rearrange("p (c d) -> p c d", c=2),
        wuv_h.rearrange("(c p) d -> p c d", c=2)), "DW", 16, ("wl", "uv"))
    emit("gpsimd", lambda e: e.dma_start(
        winb[:, :].rearrange("p (c m) -> p c m", c=8),
        win_h.rearrange("(c p) m -> p c m", c=8)), "DW", 16, ("wl", "win"))
    emit("gpsimd", lambda e: e.dma_start(woutb[:, :], wout_h), "DW", 16, ("wl", "wout"))
    emit("gpsimd", lambda e: e.dma_start(b2b[:, :], b2_h.rearrange("(p o) -> p o", o=1)),
         "DW", 16, ("wl", "b2"))
    emit("gpsimd", lambda e: e.dma_start(
        wuqb[:, :].rearrange("p (c d) -> p c d", c=8),
        wuq_h.rearrange("(c p) d -> p c d", c=8)), "DW", 16, ("wl", "uq"))

    emit("vector", lambda e: e.memset(onesc[:, :], 1.0), "V", 1, ("v", "ones"))
    emit("vector", lambda e: e.memset(vtm[:, :], 0.0), "V", 1)
    emit("vector", lambda e: e.memset(
        bass.AP(vtm, 64, [[24576, 128], [1536, 16], [192, 8], [1, 1]]), 1.0),
        "V", 1, ("v", "vtminit"))
    emit("gpsimd", lambda e: e.affine_select(
        i64[:, :], onesc[0:64, 0:64], [[1, 64]], ALU.is_equal, 0.0,
        base=0, channel_multiplier=-1), "G", 1, ("g", "i64"),
        waits=[("v", "ones")])
    for i in range(16):
        emit("vector", lambda e, i=i: e.tensor_scalar(
            diagb[0:64, 64 * i:64 * i + 64], i64[0:64, :], w2f[:, i:i + 1], None, ALU.mult),
            "V", 1, ("v", "diag", i) if i == 15 else None,
            waits=[("g", "i64"), ("ld", "smalls")])

    def xslot(t):
        return (x2 if t % 4 < 2 else x3)[:, 1024 * (t % 2):1024 * (t % 2) + 1024]
    for t in range(NCH + 1):
        sl = 1024 * (t % 2)
        slp = 1024 * ((t - 1) % 2)
        if t < NCH:
            emit("sync", lambda e, t=t: e.dma_start(
                xslot(t), x_h[128 * t:128 * t + 128, :]),
                "DX", 16, ("xload", t),
                waits=[E(("sqp", t - 4))])
            emit("scalar", lambda e, t=t, sl=sl: e.activation(
                xnb[:, sl:sl + 1024], xslot(t), AF.Copy,
                accum_out=stat[:, 0:1]),
                "S", 1, ("xsum", t),
                waits=[("xload", t), ("ld", "cons"), E(("tr", t - 2)),
                       E(("negm", t - 1))])
        else:
            emit("scalar", lambda e: e.activation(stat[:, 5:6], stat[:, 0:1], AF.Copy),
                 "S", 1)
        if t >= 1:
            emit("scalar", lambda e: e.activation(stat[:, 3:4], stat[:, 2:3], AF.Sqrt,
                                                  bias=cons[:, 2:3], scale=cons[:, 1:2]),
                 "S", 1, ("sqrt", t - 1), waits=[("ld", "cons"), E(("rcp", t - 2))])
        if t < NCH:
            emit("vector", lambda e, t=t: e.tensor_scalar(stat[:, 1:2], stat[:, 0:1],
                                                          cons[:, 0:1], None, ALU.mult),
                 "V", 1, ("negm", t), waits=[("xsum", t), E(("sqp", t - 1))])
        if t >= 1:
            emit("vector", lambda e: e.reciprocal(stat[:, 4:5], stat[:, 3:4]),
                 "V", 1, ("rcp", t - 1), waits=[("sqrt", t - 1)])
        if t < NCH:
            if t == 0:
                emit("vector", lambda e: e.tensor_copy(sq[:, 0:512], xnb[:, 0:512]), "V", 1)
            emit("vector", lambda e, sl=sl: e.tensor_scalar(
                xnb[:, sl:sl + 1024], xnb[:, sl:sl + 1024], stat[:, 1:2], None, ALU.add),
                "V", 1, ("lnxc", t))
        if t >= 1:
            emit("vector", lambda e, slp=slp: e.tensor_scalar(
                xnb[:, slp:slp + 1024], xnb[:, slp:slp + 1024], stat[:, 4:5], None, ALU.mult),
                "V", 1, ("xn", t - 1))
            emit("sync", lambda e, t=t, slp=slp: e.dma_start_transpose(
                bass.AP(xnT, 128 * (t - 1), [[16384, 128], [2048, 8], [1, 128]]),
                bass.AP(xnb, slp, [[2048, 128], [128, 8], [1, 128]])),
                "DT", 16, ("tr", t - 1), waits=[("xn", t - 1)])
        if t < NCH:
            emit("scalar", lambda e, t=t: e.activation(
                sq[:, :], xslot(t), AF.Square, bias=stat[:, 1:2],
                accum_out=stat[:, 2:3]),
                "S", 1, ("sqp", t), waits=[("negm", t)])

    slices = [SH0[:, 0:512], SH0[:, 512:1024], SH1[:, 0:512], SH1[:, 512:1024]]
    slice_ev = [None, None, None, None]
    slot = [0]

    def mm_group(tag, mms, out_slice_idx, waits):
        si = slot[0] % 4
        slot[0] += 1
        psl = slices[si]
        allw = list(waits)
        if slice_ev[si] is not None:
            allw.append(slice_ev[si])
        nmm = len(mms)
        for mi, (lf, rf) in enumerate(mms):
            emit("tensor",
                 lambda e, lf=lf, rf=rf, mi=mi, psl=psl, nmm=nmm: e.matmul(
                     psl[0:128, :], lf(), rf(), start=(mi == 0), stop=(mi == nmm - 1)),
                 "T" if mi == nmm - 1 else None, 1,
                 ("mm", tag) if mi == nmm - 1 else None,
                 waits=allw if mi == 0 else ())
        return si, psl

    def evac(tag, si, psl, out_ap_fn):
        emit("vector", lambda e, psl=psl: e.tensor_copy(out_ap_fn(), psl[0:128, :]),
             "V", 1, ("ev", tag), waits=[("mm", tag)])
        slice_ev[si] = ("ev", tag)

    for g4 in range(NG):
        trw = ("tr", 4 * g4 + 3)
        qo = 512 * g4
        for oc2 in range(2):
            tag = ("ckv", g4, oc2)
            mms = [(lambda ic=ic, oc2=oc2: wdkvb[:, 256 * ic + 128 * oc2:256 * ic + 128 * oc2 + 128],
                    lambda ic=ic, qo=qo: xnT[:, 2048 * ic + qo:2048 * ic + qo + 512])
                   for ic in range(8)]
            si, psl = mm_group(tag, mms, None, [trw, ("wl", "dkv")])
            evac(tag, si, psl,
                 lambda oc2=oc2, qo=qo: ckvT[:, 2048 * oc2 + qo:2048 * oc2 + qo + 512])
        for oc in range(8):
            tag = ("k", g4, oc)
            mms = [(lambda lc=lc, oc=oc: wukb[:, 1024 * lc + 128 * oc:1024 * lc + 128 * oc + 128],
                    lambda lc=lc, qo=qo: ckvT[:, 2048 * lc + qo:2048 * lc + qo + 512])
                   for lc in range(2)]
            si, psl = mm_group(tag, mms, None,
                               [("wl", "uk"), ("ev", ("ckv", g4, 0)), ("ev", ("ckv", g4, 1))])
            evac(tag, si, psl,
                 lambda oc=oc, qo=qo: kT[:, 2048 * oc + qo:2048 * oc + qo + 512])
        for tc in range(4 * g4, 4 * g4 + 4):
            for half in range(2):
                tag = ("v", tc, half)
                mms = [(lambda lc=lc, tc=tc: ckvT[:, 2048 * lc + 128 * tc:2048 * lc + 128 * tc + 128],
                        lambda lc=lc, half=half: wuvb[:, 1024 * lc + 512 * half:1024 * lc + 512 * half + 512])
                       for lc in range(2)]
                si, psl = mm_group(tag, mms, None,
                                   [("wl", "uv"), ("ev", ("ckv", g4, 0)), ("ev", ("ckv", g4, 1))])
                def vout(tc=tc, half=half):
                    return bass.AP(vtm, 1536 * tc + 768 * half,
                                   [[24576, 128], [192, 4], [128, 2], [1, 64]])
                emit("vector", lambda e, psl=psl, vout=vout: e.tensor_copy(
                    vout(), psl[0:128, :].rearrange("p (a r j) -> p a r j", a=4, r=2)),
                    "V", 1, ("ev", tag), waits=[("mm", tag), ("v", "vtminit")])
                slice_ev[si] = ("ev", tag)
        for oc in range(8):
            tag = ("q", g4, oc)
            mms = [(lambda ic=ic, oc=oc: wuqb[:, 1024 * ic + 128 * oc:1024 * ic + 128 * oc + 128],
                    lambda ic=ic, qo=qo: xnT[:, 2048 * ic + qo:2048 * ic + qo + 512])
                   for ic in range(8)]
            si, psl = mm_group(tag, mms, None, [("wl", "uq"), trw])
            evac(tag, si, psl,
                 lambda oc=oc, qo=qo: qT[:, 2048 * oc + qo:2048 * oc + qo + 512])

    qlast = ("mm", ("q", 3, 7))

    emit("gpsimd", lambda e: e.dma_start(
        wob[:, :].rearrange("p (c d) -> p c d", c=8),
        wo_h.rearrange("(c p) d -> p c d", c=8)), "DW", 16, ("wl", "wo"),
        waits=[qlast])

    for g4 in range(NG):
        tag = ("s", g4)
        for ic in range(8):
            emit("tensor", lambda e, ic=ic, g4=g4: e.matmul(
                C0[0:64, :], winb[:, 64 * ic:64 * ic + 64],
                xnT[:, 2048 * ic + 512 * g4:2048 * ic + 512 * g4 + 512],
                start=(ic == 0), stop=(ic == 7)),
                "T" if ic == 7 else None, 1, ("mm", tag) if ic == 7 else None,
                waits=[("wl", "win"), ("tr", 15), E(("sev", g4 - 1))] if ic == 0 else ())
        emit("vector", lambda e, g4=g4: e.tensor_scalar(
            ssb[0:64, 512 * g4:512 * g4 + 512], C0[0:64, :], binf[:, 0:1], None, ALU.add),
            "V", 1, ("sev", g4), waits=[("mm", tag), ("ld", "smalls")])
    gbank = [C0, C1, SH2[:, 0:512], SH2[:, 512:1024]]
    for i in range(16):
        hsl = 2048 * (i % 2)
        emit("scalar", lambda e, i=i, hsl=hsl: e.activation(
            hbuf[0:64, hsl:hsl + 2048], ssb[0:64, :], AF.Relu,
            bias=b1f[:, i:i + 1], scale=w1f[:, i:i + 1]),
            "S", 1, ("h", i),
            waits=[("sev", 3), E(("gmm", i - 2, 3)), ("ev", ("v", 15, 1))])
        for g4 in range(NG):
            emit("tensor", lambda e, i=i, g4=g4, hsl=hsl: e.matmul(
                gbank[g4][0:64, :], diagb[0:64, 64 * i:64 * i + 64],
                hbuf[0:64, hsl + 512 * g4:hsl + 512 * g4 + 512],
                start=(i == 0), stop=(i == 15), skip_group_check=True),
                "T", 1, ("gmm", i, g4),
                waits=[("h", i), ("v", "diag", 15)] if g4 == 0 else ())
    for g4 in range(NG):
        emit("vector", lambda e, g4=g4: e.tensor_copy(
            gbf[0:64, 512 * g4:512 * g4 + 512], gbank[g4][0:64, :]),
            "V", 1, ("gev", g4), waits=[("gmm", 15, g4)])
    for h5 in range(2):
        emit("tensor", lambda e, h5=h5: e.matmul(
            SH0[0:1, 0:512], b2b[0:64, 0:1], woutb[0:64, 512 * h5:512 * h5 + 512],
            start=True, stop=True),
            "T", 1, ("b2f", h5),
            waits=[("wl", "wout"), ("wl", "b2"), ("gev", 3), E(("bp", h5 - 1))] +
                  ([sev for sev in [slice_ev[0]] if sev is not None]))
        emit("vector", lambda e, h5=h5: e.tensor_tensor(
            boutp[0:1, 512 * h5:512 * h5 + 512], boutf[0:1, 512 * h5:512 * h5 + 512],
            SH0[0:1, 0:512], ALU.add),
            "V", 1, ("bp", h5), waits=[("b2f", h5), ("ld", "smalls")])

    if debug:
        emit("sync", lambda e: e.dma_start(dbg_xnt_early, xnT[:, :]), "DO", 16,
             ("xntdump",), waits=[("mm", ("s", 3)), ("tr", 15)])
        emit("sync", lambda e: e.dma_start(dbg_xnb, xnb[:, :]), "DO", 16, ("xnbdump",),
             waits=[("xn", 15)])
        emit("sync", lambda e: e.dma_start(dbg_x2, x2[:, :]), "DO", 16, ("x2dump",),
             waits=[("xn", 15)])
        emit("sync", lambda e: e.dma_start(dbg_stat, stat[:, :]), "DO", 16, ("statdump",),
             waits=[("xn", 15)])

    emit("vector", lambda e: e.memset(maskst[:, :], 0.0), "V", 1, ("v", "mz"),
         waits=[("mm", ("s", 3))])
    emit("gpsimd", lambda e: e.affine_select(
        i128[:, :], onesc[:, :], [[1, 128]], ALU.is_equal, 0.0,
        base=0, channel_multiplier=-1), "G", 1, waits=[("mm", ("s", 3)), ("v", "ones")])
    emit("gpsimd", lambda e: e.affine_select(
        maskst[:, 0:128], maskst[:, 0:128], [[1, 128]], ALU.is_ge, -1e30,
        base=0, channel_multiplier=-1), "G", 1, ("g", "maskinit"), waits=[("v", "mz")])

    PV = ("bp", 1)
    chunk_seq = []
    for g in range(NG):
        for p in range(NP):
            for c in range(4 * g + 4):
                chunk_seq.append((g, p, c))
    seq_of = {k: i for i, k in enumerate(chunk_seq)}

    pair_list = [(g, p) for g in range(NG) for p in range(NP)]

    def emit_pair_attention_rest(g, p):
        nch_pair = 4 * g + 4
        for c in range(2, nch_pair + 4, 2):
            if c < nch_pair:
                emit_scores(g, p, c)
                emit_scores(g, p, c + 1)
            if c >= 4:
                emit_ctx(g, p, c - 4)
                emit_ctx(g, p, c - 3)

    def emit_scores(g, p, c):
            seq = seq_of[(g, p, c)]
            j = c - 4 * g
            st = 128 * j if j > 0 else 0
            sh = SH[seq % 3]
            r = seq % RING
            diag = (j >= 0)
            w0 = [E(("exp", seq - 3))]
            if seq == 0:
                w0 += [PV, ("gev", 3), ("sev", 3), ("g", "maskinit"), E(("xntdump",))]
            emit("tensor", lambda e, sh=sh, p=p, c=c, g=g, st=st, diag=diag: e.matmul(
                sh[:, st:512], kT[0:64, 2048 * p + 128 * c:2048 * p + 128 * c + 128],
                qT[0:64, 2048 * p + 512 * g + st:2048 * p + 512 * g + 512],
                start=True, stop=not diag, tile_position=(0, 0), skip_group_check=True),
                None, waits=w0)
            emit("tensor", lambda e, sh=sh, p=p, c=c, g=g, st=st, diag=diag: e.matmul(
                sh[:, 512 + st:1024], kT[64:128, 2048 * p + 128 * c:2048 * p + 128 * c + 128],
                qT[64:128, 2048 * p + 512 * g + st:2048 * p + 512 * g + 512],
                start=True, stop=not diag, tile_position=(64, 0), skip_group_check=True),
                None if diag else "T", 1, None if diag else ("sc", seq))
            if diag:
                emit("tensor", lambda e, sh=sh, st=st: e.matmul(
                    sh[:, st:st + 128], i128[:, :], maskst[:, 0:128],
                    start=False, stop=True, skip_group_check=True), None)
                emit("tensor", lambda e, sh=sh, st=st: e.matmul(
                    sh[:, 512 + st:512 + st + 128], i128[:, :], maskst[:, 0:128],
                    start=False, stop=True, skip_group_check=True),
                    "T", 1, ("sc", seq))
            if debug and seq == 0:
                emit("vector", lambda e, sh=sh: e.tensor_copy(lnb[:, :], sh[:, 0:512]),
                     "V", 1, waits=[("sc", seq)])
                emit("vector", lambda e, sh=sh: e.tensor_copy(stag[:, 0:512], sh[:, 512:1024]),
                     "V", 1, ("shcap", seq))
                emit("sync", lambda e: e.dma_start(dbg_sh[:, 0:512], lnb[:, :]),
                     "DO", 16, waits=[("shcap", seq)])
                emit("sync", lambda e: e.dma_start(dbg_sh[:, 512:1024], stag[:, 0:512]),
                     "DO", 16, ("shdump",))
            expw = [("sc", seq), E(("ctx", seq - RING))]
            if seq == 0:
                expw.append(("ld", "cons"))
                expw.append(E(("shcap", 0)))
            if seq == RING:
                expw.append(E(("expdump",)))
            if st == 0:
                emit("scalar", lambda e, sh=sh, r=r: e.activation(
                    expT[:, 1024 * r:1024 * r + 1024], sh[:, 0:1024], AF.Exp,
                    scale=0.125),
                    "S", 1, ("exp", seq), waits=expw)
            else:
                emit("scalar", lambda e, sh=sh, r=r, st=st: e.activation(
                    expT[:, 1024 * r + st:1024 * r + 512], sh[:, st:512], AF.Exp,
                    scale=0.125), None, waits=expw)
                emit("scalar", lambda e, sh=sh, r=r, st=st: e.activation(
                    expT[:, 1024 * r + 512 + st:1024 * r + 1024], sh[:, 512 + st:1024],
                    AF.Exp, scale=0.125),
                    "S", 1, ("exp", seq))
            emit_stairs(g, p, c)
    def emit_stairs(g, p, c):
            return

    def emit_ctx(g, p, c):
            seq = seq_of[(g, p, c)]
            j = c - 4 * g
            st = 128 * j if j > 0 else 0
            r = seq % RING
            ctxw = [("exp", seq)]
            if c == 0:
                prev = pair_list.index((g, p)) - 1
                if prev >= 0:
                    ctxw.append(("cev", pair_list[prev]))
                    if prev >= 1:
                        ctxw.append(E(("norm", pair_list[prev - 1])))
                    ctxw.append(last_oev[0])
                else:
                    ctxw.append(PV)
            vb = 1536 * c + 192 * p
            emit("tensor", lambda e, vb=vb, r=r, st=st, c=c, g=g: e.matmul(
                C0[0:65, st:512], vtm[:, vb:vb + 65],
                expT[:, 1024 * r + st:1024 * r + 512],
                start=(c == 0), stop=(c == 4 * g + 3), skip_group_check=True),
                None, waits=ctxw)
            emit("tensor", lambda e, vb=vb, r=r, st=st, c=c, g=g: e.matmul(
                C1[0:128, st:512], vtm[:, vb + 64:vb + 192],
                expT[:, 1024 * r + 512 + st:1024 * r + 1024],
                start=(c == 0), stop=(c == 4 * g + 3), skip_group_check=True),
                "T", 1, ("ctx", seq))

    def emit_pair_evac(g, p):
        pb = 512 * (p % 2)
        pk = (g, p)
        idx = pair_list.index(pk)
        wv = [("ctx", seq_of[(g, p, 4 * g + 3)])]
        if idx >= 2:
            wv.append(("recip", pair_list[idx - 2]))
        if idx == 2:
            wv.append(E(("stagdump",)))
            wv.append(E(("recbdump",)))
        emit("vector", lambda e, pb=pb: e.tensor_copy(stag[64:65, pb:pb + 512], C0[64:65, 0:512]),
             "V", 1, waits=wv + ([E(("shdump",))] if pk == (0, 0) else []))
        emit("vector", lambda e, pb=pb: e.tensor_copy(stag[0:1, pb:pb + 512], C1[0:1, 0:512]),
             "V", 1, ("scp", pk))
        co = 4096 * (g % 2) + 512 * p
        wv2 = []
        if g >= 2:
            wv2.append(("bmm", (4 * (g - 2) + 3, 1)))
        if g == 2 and p == 0:
            wv2.append(E(("ctxg0dump",)))
        emit("vector", lambda e, co=co: e.tensor_copy(ctxT2[0:64, co:co + 512], C0[0:64, 0:512]),
             "V", 1, waits=wv2)
        emit("vector", lambda e, co=co: e.tensor_copy(ctxT2[64:128, co:co + 512], C1[64:128, 0:512]),
             "V", 1, ("cev", pk))

    def emit_pair_tail(g, p):
        pk = (g, p)
        idx = pair_list.index(pk)
        pb = 512 * (p % 2)
        emit("scalar", lambda e, pb=pb: e.activation(lnb[0:65, :], stag[0:65, pb:pb + 512], AF.Ln),
             "S", 1, waits=[("scp", pk), E(("shdump",))])
        wrec = [E(("bc", pair_list[idx - 2]))] if idx >= 2 else []
        emit("scalar", lambda e, pb=pb: e.activation(
            recb[0:65, pb:pb + 512], lnb[0:65, :], AF.Exp, scale=-1.0),
            "S", 1, ("recip", pk), waits=wrec)
        wb = [("recip", pk), ("cev", pk)]
        if idx + 1 < len(pair_list) and ("cev", pair_list[idx + 1]) in ev:
            wb.append(("cev", pair_list[idx + 1]))
        if idx >= 1:
            wb.append(("norm", pair_list[idx - 1]))
        else:
            wb.append(("bp", 1))
        emit("tensor", lambda e, pb=pb: e.matmul(
            BB[0:64, :], onesc[64:65, 0:64], recb[64:65, pb:pb + 512], start=True, stop=True),
            None, waits=wb)
        emit("tensor", lambda e, pb=pb: e.matmul(
            BB[64:128, :], onesc[0:1, 0:64], recb[0:1, pb:pb + 512], start=True, stop=True),
            "T", 1, ("bc", pk))
        co = 4096 * (g % 2) + 512 * p
        emit("vector", lambda e, co=co: e.tensor_tensor(
            ctxT2[:, co:co + 512], ctxT2[:, co:co + 512], BB[0:128, 0:512], ALU.mult),
            "V", 1, ("norm", pk), waits=[("bc", pk), ("cev", pk)])

    last_oev = [None]

    def emit_burst(gm1, bi):
        tc = 4 * gm1 + bi // 2
        half = bi % 2
        gbuf = gm1 % 2
        tl = tc % 4
        if gm1 == 0 and bi == 0:
            emit_x2_load(0)
            emit_x2_load(1)
        wv = [("norm", (gm1, 7)), ("wl", "wo"), last_oev[0], ("gev", 2)]
        wv = [w for w in wv if w is not None]
        for cc in range(8):
            emit("tensor", lambda e, cc=cc, gbuf=gbuf, tl=tl, half=half: e.matmul(
                WOB[0:128, :], ctxT2[:, 4096 * gbuf + 512 * cc + 128 * tl:4096 * gbuf + 512 * cc + 128 * tl + 128],
                wob[:, 1024 * cc + 512 * half:1024 * cc + 512 * half + 512],
                start=(cc == 0), stop=False, skip_group_check=True),
                None, waits=wv if cc == 0 else ())
        emit("tensor", lambda e, tc=tc, half=half: e.matmul(
            WOB[0:128, :], gbf[0:64, 128 * tc:128 * tc + 128],
            woutb[0:64, 512 * half:512 * half + 512],
            start=False, stop=False, skip_group_check=True), None)
        emit("tensor", lambda e, half=half: e.matmul(
            WOB[0:128, :], onesc[0:1, 0:128], boutp[0:1, 512 * half:512 * half + 512],
            start=False, stop=True, skip_group_check=True),
            "T", 1, ("bmm", (tc, half)))
        xoff = 1024 * (tc % 2) + 512 * half
        emit("vector", lambda e, xoff=xoff: e.tensor_tensor(
            outb[:, xoff:xoff + 512], x2[:, xoff:xoff + 512], WOB[0:128, 0:512], ALU.add),
            "V", 1, ("oev", (tc, half)),
            waits=[("bmm", (tc, half)), ("x2", tc), E(("ost", (tc - 2, half)))])
        last_oev[0] = ("oev", (tc, half))
        emit("sync", lambda e, tc=tc, half=half, xoff=xoff: e.dma_start(
            y_h[128 * tc:128 * tc + 128, 512 * half:512 * half + 512],
            outb[:, xoff:xoff + 512]),
            "DO", 16, ("ost", (tc, half)), waits=[("oev", (tc, half))])
        if half == 1 and tc + 2 <= 15:
            emit_x2_load(tc + 2)

    def emit_x2_load(tc):
        wv = [("lnxc", 15), E(("x2dump",))]
        if tc >= 2:
            wv.append(("oev", (tc - 2, 1)))
        emit("sync", lambda e, tc=tc: e.dma_start(
            x2[:, 1024 * (tc % 2):1024 * (tc % 2) + 1024], x_h[128 * tc:128 * tc + 128, :]),
            "DX", 16, ("x2", tc), waits=wv)

    all_bursts = [(gm1, b) for gm1 in range(4) for b in range(8)]
    for idx, (g, p) in enumerate(pair_list):
        emit_scores(g, p, 0)
        emit_scores(g, p, 1)
        if idx >= 2:
            emit_pair_tail(*pair_list[idx - 2])
        bk = idx - 9
        if 0 <= bk < 24:
            emit_burst(*all_bursts[bk])
        emit_pair_attention_rest(g, p)
        emit_pair_evac(g, p)
        if debug and idx == 0:
            emit("sync", lambda e: e.dma_start(dbg_exp00, expT[:, :]), "DO", 16,
                 ("expdump",), waits=[("ctx", seq_of[(0, 0, 3)])])
        if debug and idx == 1:
            emit("sync", lambda e: e.dma_start(dbg_stag00, stag[:, :]), "DO", 16,
                 ("stagdump",), waits=[("scp", (0, 0)), ("recip", (0, 0))])
            emit("sync", lambda e: e.dma_start(dbg_recb00, recb[:, :]), "DO", 16, ("recbdump",))
        if debug and idx == 8:
            emit("sync", lambda e: e.dma_start(dbg_ctxg0, ctxT2[:, 0:4096]), "DO", 16,
                 ("ctxg0dump",), waits=[("norm", (0, 7))])
    emit_pair_tail(3, 6)
    emit_pair_tail(3, 7)
    for bk in range(23, 32):
        emit_burst(*all_bursts[bk])

    if debug:
        emit("sync", lambda e: e.dma_start(dbg_kT, kT[:, :]), "DO", 16,
             waits=[last_oev[0]])
        emit("sync", lambda e: e.dma_start(dbg_qT, qT[:, :]), "DO", 16)
        emit("sync", lambda e: e.dma_start(dbg_ctx, ctxT2[:, :]), "DO", 16)
        emit("sync", lambda e: e.dma_start(dbg_vtm, vtm[:, :]), "DO", 16)
        emit("sync", lambda e: e.dma_start(dbg_xnt, xnT[:, :]), "DO", 16)

    nstores = cnt["DO"]

    with nc.Block() as block:
        @block.sync
        def _(sync):
            for f in ops["sync"]:
                f(sync)
            sync.wait_ge(semh["DO"], nstores)

        @block.tensor
        def _(tensor):
            for f in ops["tensor"]:
                f(tensor)

        @block.scalar
        def _(scalar):
            for f in ops["scalar"]:
                f(scalar)

        @block.vector
        def _(vector):
            for f in ops["vector"]:
                f(vector)

        @block.gpsimd
        def _(gpsimd):
            for f in ops["gpsimd"]:
                f(gpsimd)

    for c in ctxs:
        pass
    return nc


_NC_CACHE = None


def kernel(L0_attn_x, L0_attn_W_DKV, L0_attn_W_UK, L0_attn_W_UV, L0_attn_W_UQ,
           L0_attn_W_O, L0_ffn_Win, L0_ffn_bin_, L0_ffn_W1, L0_ffn_b1, L0_ffn_W2,
           L0_ffn_b2, L0_ffn_Wout, L0_ffn_bout):
    global _NC_CACHE
    x = np.ascontiguousarray(np.asarray(L0_attn_x, dtype=np.float32))
    consts = np.zeros((128, 8), np.float32)
    consts[:, 0] = -1.0 / 1024.0
    consts[:, 1] = 1.0 / 1024.0
    consts[:, 2] = 1e-5
    consts[:, 3] = 0.125
    consts[:, 4] = -1.0
    base = {
        "W_DKV": np.ascontiguousarray(L0_attn_W_DKV, dtype=np.float32),
        "W_UK": np.ascontiguousarray(L0_attn_W_UK, dtype=np.float32),
        "W_UV": np.ascontiguousarray(L0_attn_W_UV, dtype=np.float32),
        "W_UQ": np.ascontiguousarray(L0_attn_W_UQ, dtype=np.float32),
        "W_O": np.ascontiguousarray(L0_attn_W_O, dtype=np.float32),
        "Win": np.ascontiguousarray(L0_ffn_Win, dtype=np.float32),
        "bin_": np.ascontiguousarray(L0_ffn_bin_, dtype=np.float32),
        "W1": np.ascontiguousarray(L0_ffn_W1, dtype=np.float32),
        "b1": np.ascontiguousarray(L0_ffn_b1, dtype=np.float32),
        "W2": np.ascontiguousarray(L0_ffn_W2, dtype=np.float32),
        "b2": np.ascontiguousarray(L0_ffn_b2, dtype=np.float32),
        "Wout": np.ascontiguousarray(L0_ffn_Wout, dtype=np.float32),
        "bout": np.ascontiguousarray(L0_ffn_bout, dtype=np.float32),
        "consts": consts,
    }
    in_maps = []
    for core in range(8):
        m = dict(base)
        m["x"] = np.ascontiguousarray(x[core])
        in_maps.append(m)
    if _NC_CACHE is None:
        _NC_CACHE = build_core()
    res = run_bass_kernel_spmd(_NC_CACHE, in_maps, core_ids=list(range(8)))
    return np.stack([res.results[i]["y"] for i in range(8)], axis=0)


# version 17
# speedup vs baseline: 1.0150x; 1.0028x over previous
import numpy as np
import concourse.bass as bass
import concourse.mybir as mybir
from concourse.bass_utils import run_bass_kernel_spmd

F32 = mybir.dt.float32
BF16 = mybir.dt.bfloat16
AF = mybir.ActivationFunctionType
ALU = mybir.AluOpType
AX = mybir.AxisListType

T = 2048
D = 1024
NCH = 16
NG = 4
NP = 8
RING = 8

OFF_X2 = 16384
OFF_XNB = 24576
OFF_SQ = 28672
OFF_CONST = 30720
OFF_STAT = 30752
OFF_ONES = 30784
OFF_W1 = 31040
OFF_B1 = 31104
OFF_W2 = 31168
OFF_BIN = 31232
OFF_B2B = 31264
OFF_I64 = 31296
OFF_BOUTF = 31424
OFF_BOUTP = 35520
OFF_XNT = 37568
OFF_EXPT = 37568
OFF_STAG = 58048
OFF_LNB = 62144
OFF_RECIP = 64192
OFF_GBF = 53952
OFF_MASKST = 66240
OFF_I128 = 67264
OFF_QT = 70336
OFF_KT = 103104
OFF_VTM = 135872
OFF_WDKV = 185024
OFF_WUK = 189120
OFF_WUV = 193216
OFF_WIN = 197312
OFF_DIAG = 198336
OFF_SSB = 185024
OFF_CTXT = 185024
OFF_WUQ = 201408
OFF_WO = 201408
OFF_WOUT = 217792
OFF_CKVT = 219840
OFF_H = 219840
OFF_OUT = 219840


def build_core(debug=False):
    nc = bass.Bass("TRN2", target_bir_lowering=False, detect_race_conditions=False)

    x_h = nc.dram_tensor("x", [T, D], F32, kind="ExternalInput").ap()
    wdkv_h = nc.dram_tensor("W_DKV", [D, 256], F32, kind="ExternalInput").ap()
    wuk_h = nc.dram_tensor("W_UK", [256, D], F32, kind="ExternalInput").ap()
    wuv_h = nc.dram_tensor("W_UV", [256, D], F32, kind="ExternalInput").ap()
    wuq_h = nc.dram_tensor("W_UQ", [D, D], F32, kind="ExternalInput").ap()
    wo_h = nc.dram_tensor("W_O", [D, D], F32, kind="ExternalInput").ap()
    win_h = nc.dram_tensor("Win", [D, 64], F32, kind="ExternalInput").ap()
    bin_h = nc.dram_tensor("bin_", [64], F32, kind="ExternalInput").ap()
    w1_h = nc.dram_tensor("W1", [64, 16], F32, kind="ExternalInput").ap()
    b1_h = nc.dram_tensor("b1", [64, 16], F32, kind="ExternalInput").ap()
    w2_h = nc.dram_tensor("W2", [64, 16], F32, kind="ExternalInput").ap()
    b2_h = nc.dram_tensor("b2", [64], F32, kind="ExternalInput").ap()
    wout_h = nc.dram_tensor("Wout", [64, D], F32, kind="ExternalInput").ap()
    bout_h = nc.dram_tensor("bout", [1024], F32, kind="ExternalInput").ap()
    consts_h = nc.dram_tensor("consts", [128, 8], F32, kind="ExternalInput").ap()
    y_h = nc.dram_tensor("y", [T, D], F32, kind="ExternalOutput").ap()
    if debug:
        dbg_exp00 = nc.dram_tensor("dbg_exp00", [128, 4096], BF16, kind="ExternalOutput").ap()
        dbg_xnt_early = nc.dram_tensor("dbg_xnt_early", [128, 16384], BF16, kind="ExternalOutput").ap()
        dbg_xnb = nc.dram_tensor("dbg_xnb", [128, 2048], BF16, kind="ExternalOutput").ap()
        dbg_sh = nc.dram_tensor("dbg_sh", [128, 1024], F32, kind="ExternalOutput").ap()
        dbg_x2 = nc.dram_tensor("dbg_x2", [128, 2048], F32, kind="ExternalOutput").ap()
        dbg_stat = nc.dram_tensor("dbg_stat", [128, 8], F32, kind="ExternalOutput").ap()
        dbg_stag00 = nc.dram_tensor("dbg_stag00", [128, 1024], F32, kind="ExternalOutput").ap()
        dbg_recb00 = nc.dram_tensor("dbg_recb00", [128, 1024], BF16, kind="ExternalOutput").ap()
        dbg_ctxg0 = nc.dram_tensor("dbg_ctxg0", [128, 4096], BF16, kind="ExternalOutput").ap()
        dbg_kT = nc.dram_tensor("dbg_kT", [128, 16384], BF16, kind="ExternalOutput").ap()
        dbg_qT = nc.dram_tensor("dbg_qT", [128, 16384], BF16, kind="ExternalOutput").ap()
        dbg_ctx = nc.dram_tensor("dbg_ctx", [128, 8192], BF16, kind="ExternalOutput").ap()
        dbg_vtm = nc.dram_tensor("dbg_vtm", [128, 24576], BF16, kind="ExternalOutput").ap()
        dbg_xnt = nc.dram_tensor("dbg_xnt", [128, 16384], BF16, kind="ExternalOutput").ap()

    arena = nc.alloc_sbuf_tensor("manual_arena", [128, 211648], mybir.dt.uint8)
    base = nc.lookup_mloc(arena).addr

    def sb(name, shape, dtype, off):
        return nc.alloc_sbuf_tensor_at(name, shape, dtype, offset=base + off - 16384)

    x2 = sb("x2", [128, 2048], F32, OFF_X2)
    x3 = sb("x3", [128, 2048], F32, OFF_OUT)
    xnb = sb("xnb", [128, 2048], BF16, OFF_XNB)
    sq = sb("sq", [128, 1024], BF16, OFF_SQ)
    cons = sb("cons", [128, 8], F32, OFF_CONST)
    stat = sb("stat", [128, 8], F32, OFF_STAT)
    onesc = sb("onesc", [128, 128], BF16, OFF_ONES)
    w1f = sb("w1f", [64, 16], F32, OFF_W1)
    b1f = sb("b1f", [64, 16], F32, OFF_B1)
    w2f = sb("w2f", [64, 16], F32, OFF_W2)
    binf = sb("binf", [64, 1], F32, OFF_BIN)
    b2b = sb("b2b", [64, 1], BF16, OFF_B2B)
    i64 = sb("i64", [64, 64], BF16, OFF_I64)
    boutf = sb("boutf", [1, 1024], F32, OFF_BOUTF)
    boutp = sb("boutp", [1, 1024], BF16, OFF_BOUTP)
    xnT = sb("xnT", [128, 16384], BF16, OFF_XNT)
    expT = sb("expT", [128, 8192], BF16, OFF_EXPT)
    stag = sb("stag", [128, 1024], F32, OFF_STAG)
    lnb = sb("lnb", [128, 512], F32, OFF_LNB)
    recb = sb("recb", [128, 1024], BF16, OFF_RECIP)
    gbf = sb("gbf", [64, 2048], BF16, OFF_GBF)
    maskst = sb("maskst", [128, 512], BF16, OFF_MASKST)
    i128 = sb("i128", [128, 128], BF16, OFF_I128)
    qT = sb("qT", [128, 16384], BF16, OFF_QT)
    kT = sb("kT", [128, 16384], BF16, OFF_KT)
    vtm = sb("vtm", [128, 24576], BF16, OFF_VTM)
    wdkvb = sb("wdkvb", [128, 2048], BF16, OFF_WDKV)
    wukb = sb("wukb", [128, 2048], BF16, OFF_WUK)
    wuvb = sb("wuvb", [128, 2048], BF16, OFF_WUV)
    winb = sb("winb", [128, 512], BF16, OFF_WIN)
    diagb = sb("diagb", [64, 1024], BF16, OFF_DIAG)
    ssb = sb("ssb", [64, 2048], F32, OFF_SSB)
    ctxT2 = sb("ctxT2", [128, 8192], BF16, OFF_CTXT)
    wuqb = sb("wuqb", [128, 8192], BF16, OFF_WUQ)
    wob = sb("wob", [128, 8192], BF16, OFF_WO)
    woutb = sb("woutb", [64, 1024], BF16, OFF_WOUT)
    ckvT = sb("ckvT", [128, 4096], BF16, OFF_CKVT)
    hbuf = sb("hbuf", [64, 4096], BF16, OFF_H)
    outb = sb("outb", [128, 2048], F32, OFF_OUT)

    SH0 = nc.alloc_psum_tensor("SH0", [128, 1024], F32)
    SH1 = nc.alloc_psum_tensor("SH1", [128, 1024], F32)
    SH2 = nc.alloc_psum_tensor("SH2", [128, 1024], F32)
    C0 = nc.alloc_psum_tensor("C0", [128, 512], F32)
    C1 = nc.alloc_psum_tensor("C1", [128, 512], F32)
    WOB = C0
    BB = C1
    SH = [SH0, SH1, SH2]

    sems = {}
    ops = {"tensor": [], "scalar": [], "vector": [], "gpsimd": [], "sync": []}
    cnt = {}
    ev = {}

    ctxs = []
    for name in ["T", "S", "V", "G", "DX", "DT", "DW", "DO"]:
        ctxs.append(nc.semaphore(name))
    semh = {}
    for name, c in zip(["T", "S", "V", "G", "DX", "DT", "DW", "DO"], ctxs):
        semh[name] = c.__enter__()
        cnt[name] = 0

    def emit(eng, fn, sem=None, n=1, event=None, waits=()):
        wmap = {}
        for key in waits:
            if key is None:
                continue
            s, v = ev[key]
            wmap[s] = max(wmap.get(s, 0), v)
        wlist = [(semh[s], v) for s, v in wmap.items()]
        if sem is not None:
            cnt[sem] += n
            if event is not None:
                ev[event] = (sem, cnt[sem])
            semarg = (semh[sem], n)
        else:
            semarg = None

        def closure(e, fn=fn, wlist=wlist, semarg=semarg):
            for sh_, v in wlist:
                e.wait_ge(sh_, v)
            ins = fn(e)
            if semarg is not None:
                ins.then_inc(semarg[0], semarg[1])
        ops[eng].append(closure)

    def E(key):
        return key if key in ev else None

    emit("sync", lambda e: e.dma_start(cons[:, :], consts_h), "DX", 16, ("ld", "cons"))
    emit("sync", lambda e: e.dma_start(w1f[:, :], w1_h), "DX", 16)
    emit("sync", lambda e: e.dma_start(b1f[:, :], b1_h), "DX", 16)
    emit("sync", lambda e: e.dma_start(w2f[:, :], w2_h), "DX", 16)
    emit("sync", lambda e: e.dma_start(binf[:, :], bin_h.rearrange("(p o) -> p o", o=1)), "DX", 16)
    emit("sync", lambda e: e.dma_start(boutf[:, :], bout_h.rearrange("(o f) -> o f", o=1)),
         "DX", 16, ("ld", "smalls"))

    emit("gpsimd", lambda e: e.dma_start(
        wdkvb[:, :].rearrange("p (c l) -> p c l", c=8),
        wdkv_h.rearrange("(c p) l -> p c l", c=8)), "DW", 16, ("wl", "dkv"))
    emit("gpsimd", lambda e: e.dma_start(
        wukb[:, :].rearrange("p (c d) -> p c d", c=2),
        wuk_h.rearrange("(c p) d -> p c d", c=2)), "DW", 16, ("wl", "uk"))
    emit("gpsimd", lambda e: e.dma_start(
        wuvb[:, :].rearrange("p (c d) -> p c d", c=2),
        wuv_h.rearrange("(c p) d -> p c d", c=2)), "DW", 16, ("wl", "uv"))
    emit("gpsimd", lambda e: e.dma_start(
        winb[:, :].rearrange("p (c m) -> p c m", c=8),
        win_h.rearrange("(c p) m -> p c m", c=8)), "DW", 16, ("wl", "win"))
    emit("gpsimd", lambda e: e.dma_start(woutb[:, :], wout_h), "DW", 16, ("wl", "wout"))
    emit("gpsimd", lambda e: e.dma_start(b2b[:, :], b2_h.rearrange("(p o) -> p o", o=1)),
         "DW", 16, ("wl", "b2"))
    emit("gpsimd", lambda e: e.dma_start(
        wuqb[:, :].rearrange("p (c d) -> p c d", c=8),
        wuq_h.rearrange("(c p) d -> p c d", c=8)), "DW", 16, ("wl", "uq"))

    emit("vector", lambda e: e.memset(onesc[:, :], 1.0), "V", 1, ("v", "ones"))
    emit("vector", lambda e: e.memset(vtm[:, :], 0.0), "V", 1)
    emit("vector", lambda e: e.memset(
        bass.AP(vtm, 64, [[24576, 128], [1536, 16], [192, 8], [1, 1]]), 1.0),
        "V", 1, ("v", "vtminit"))
    emit("gpsimd", lambda e: e.affine_select(
        i64[:, :], onesc[0:64, 0:64], [[1, 64]], ALU.is_equal, 0.0,
        base=0, channel_multiplier=-1), "G", 1, ("g", "i64"),
        waits=[("v", "ones")])
    for i in range(16):
        emit("vector", lambda e, i=i: e.tensor_scalar(
            diagb[0:64, 64 * i:64 * i + 64], i64[0:64, :], w2f[:, i:i + 1], None, ALU.mult),
            "V", 1, ("v", "diag", i) if i == 15 else None,
            waits=[("g", "i64"), ("ld", "smalls")])

    def xslot(t):
        return (x2 if t % 4 < 2 else x3)[:, 1024 * (t % 2):1024 * (t % 2) + 1024]
    for t in range(NCH + 1):
        sl = 1024 * (t % 2)
        slp = 1024 * ((t - 1) % 2)
        if t < NCH:
            emit("sync", lambda e, t=t: e.dma_start(
                xslot(t), x_h[128 * t:128 * t + 128, :]),
                "DX", 16, ("xload", t),
                waits=[E(("sqp", t - 4))])
            emit("scalar", lambda e, t=t, sl=sl: e.activation(
                xnb[:, sl:sl + 1024], xslot(t), AF.Copy,
                accum_out=stat[:, 0:1]),
                "S", 1, ("xsum", t),
                waits=[("xload", t), ("ld", "cons"), E(("tr", t - 2)),
                       E(("negm", t - 1))])
        else:
            emit("scalar", lambda e: e.activation(stat[:, 5:6], stat[:, 0:1], AF.Copy),
                 "S", 1)
        if t >= 1:
            emit("scalar", lambda e: e.activation(stat[:, 3:4], stat[:, 2:3], AF.Sqrt,
                                                  bias=cons[:, 2:3], scale=cons[:, 1:2]),
                 "S", 1, ("sqrt", t - 1), waits=[("ld", "cons"), E(("rcp", t - 2))])
        if t < NCH:
            emit("vector", lambda e, t=t: e.tensor_scalar(stat[:, 1:2], stat[:, 0:1],
                                                          cons[:, 0:1], None, ALU.mult),
                 "V", 1, ("negm", t), waits=[("xsum", t), E(("sqp", t - 1))])
        if t >= 1:
            emit("vector", lambda e: e.reciprocal(stat[:, 4:5], stat[:, 3:4]),
                 "V", 1, ("rcp", t - 1), waits=[("sqrt", t - 1)])
        if t < NCH:
            if t == 0:
                emit("vector", lambda e: e.tensor_copy(sq[:, 0:512], xnb[:, 0:512]), "V", 1)
            emit("vector", lambda e, sl=sl: e.tensor_scalar(
                xnb[:, sl:sl + 1024], xnb[:, sl:sl + 1024], stat[:, 1:2], None, ALU.add),
                "V", 1, ("lnxc", t))
        if t >= 1:
            emit("vector", lambda e, slp=slp: e.tensor_scalar(
                xnb[:, slp:slp + 1024], xnb[:, slp:slp + 1024], stat[:, 4:5], None, ALU.mult),
                "V", 1, ("xn", t - 1))
            emit("sync", lambda e, t=t, slp=slp: e.dma_start_transpose(
                bass.AP(xnT, 128 * (t - 1), [[16384, 128], [2048, 8], [1, 128]]),
                bass.AP(xnb, slp, [[2048, 128], [128, 8], [1, 128]])),
                "DT", 16, ("tr", t - 1), waits=[("xn", t - 1)])
        if t < NCH:
            emit("scalar", lambda e, t=t: e.activation(
                sq[:, :], xslot(t), AF.Square, bias=stat[:, 1:2],
                accum_out=stat[:, 2:3]),
                "S", 1, ("sqp", t), waits=[("negm", t)])

    slices = [SH0[:, 0:512], SH0[:, 512:1024], SH1[:, 0:512], SH1[:, 512:1024]]
    slice_ev = [None, None, None, None]
    slot = [0]

    def mm_group(tag, mms, out_slice_idx, waits):
        si = slot[0] % 4
        slot[0] += 1
        psl = slices[si]
        allw = list(waits)
        if slice_ev[si] is not None:
            allw.append(slice_ev[si])
        nmm = len(mms)
        for mi, (lf, rf) in enumerate(mms):
            emit("tensor",
                 lambda e, lf=lf, rf=rf, mi=mi, psl=psl, nmm=nmm: e.matmul(
                     psl[0:128, :], lf(), rf(), start=(mi == 0), stop=(mi == nmm - 1)),
                 "T" if mi == nmm - 1 else None, 1,
                 ("mm", tag) if mi == nmm - 1 else None,
                 waits=allw if mi == 0 else ())
        return si, psl

    def evac(tag, si, psl, out_ap_fn):
        emit("vector", lambda e, psl=psl: e.tensor_copy(out_ap_fn(), psl[0:128, :]),
             "V", 1, ("ev", tag), waits=[("mm", tag)])
        slice_ev[si] = ("ev", tag)

    for g4 in range(NG):
        trw = ("tr", 4 * g4 + 3)
        qo = 512 * g4
        for oc2 in range(2):
            tag = ("ckv", g4, oc2)
            mms = [(lambda ic=ic, oc2=oc2: wdkvb[:, 256 * ic + 128 * oc2:256 * ic + 128 * oc2 + 128],
                    lambda ic=ic, qo=qo: xnT[:, 2048 * ic + qo:2048 * ic + qo + 512])
                   for ic in range(8)]
            si, psl = mm_group(tag, mms, None, [trw, ("wl", "dkv")])
            evac(tag, si, psl,
                 lambda oc2=oc2, qo=qo: ckvT[:, 2048 * oc2 + qo:2048 * oc2 + qo + 512])
        for oc in range(8):
            tag = ("k", g4, oc)
            mms = [(lambda lc=lc, oc=oc: wukb[:, 1024 * lc + 128 * oc:1024 * lc + 128 * oc + 128],
                    lambda lc=lc, qo=qo: ckvT[:, 2048 * lc + qo:2048 * lc + qo + 512])
                   for lc in range(2)]
            si, psl = mm_group(tag, mms, None,
                               [("wl", "uk"), ("ev", ("ckv", g4, 0)), ("ev", ("ckv", g4, 1))])
            evac(tag, si, psl,
                 lambda oc=oc, qo=qo: kT[:, 2048 * oc + qo:2048 * oc + qo + 512])
        for tc in range(4 * g4, 4 * g4 + 4):
            for half in range(2):
                tag = ("v", tc, half)
                mms = [(lambda lc=lc, tc=tc: ckvT[:, 2048 * lc + 128 * tc:2048 * lc + 128 * tc + 128],
                        lambda lc=lc, half=half: wuvb[:, 1024 * lc + 512 * half:1024 * lc + 512 * half + 512])
                       for lc in range(2)]
                si, psl = mm_group(tag, mms, None,
                                   [("wl", "uv"), ("ev", ("ckv", g4, 0)), ("ev", ("ckv", g4, 1))])
                def vout(tc=tc, half=half):
                    return bass.AP(vtm, 1536 * tc + 768 * half,
                                   [[24576, 128], [192, 4], [128, 2], [1, 64]])
                emit("vector", lambda e, psl=psl, vout=vout: e.tensor_copy(
                    vout(), psl[0:128, :].rearrange("p (a r j) -> p a r j", a=4, r=2)),
                    "V", 1, ("ev", tag), waits=[("mm", tag), ("v", "vtminit")])
                slice_ev[si] = ("ev", tag)
        for oc in range(8):
            tag = ("q", g4, oc)
            mms = [(lambda ic=ic, oc=oc: wuqb[:, 1024 * ic + 128 * oc:1024 * ic + 128 * oc + 128],
                    lambda ic=ic, qo=qo: xnT[:, 2048 * ic + qo:2048 * ic + qo + 512])
                   for ic in range(8)]
            si, psl = mm_group(tag, mms, None, [("wl", "uq"), trw])
            evac(tag, si, psl,
                 lambda oc=oc, qo=qo: qT[:, 2048 * oc + qo:2048 * oc + qo + 512])

    qlast = ("mm", ("q", 3, 7))

    emit("gpsimd", lambda e: e.dma_start(
        wob[:, :].rearrange("p (c d) -> p c d", c=8),
        wo_h.rearrange("(c p) d -> p c d", c=8)), "DW", 16, ("wl", "wo"),
        waits=[qlast])

    for g4 in range(NG):
        tag = ("s", g4)
        for ic in range(8):
            emit("tensor", lambda e, ic=ic, g4=g4: e.matmul(
                C0[0:64, :], winb[:, 64 * ic:64 * ic + 64],
                xnT[:, 2048 * ic + 512 * g4:2048 * ic + 512 * g4 + 512],
                start=(ic == 0), stop=(ic == 7)),
                "T" if ic == 7 else None, 1, ("mm", tag) if ic == 7 else None,
                waits=[("wl", "win"), ("tr", 15), E(("sev", g4 - 1))] if ic == 0 else ())
        emit("vector", lambda e, g4=g4: e.tensor_scalar(
            ssb[0:64, 512 * g4:512 * g4 + 512], C0[0:64, :], binf[:, 0:1], None, ALU.add),
            "V", 1, ("sev", g4), waits=[("mm", tag), ("ld", "smalls")])
    gbank = [C0, C1, SH2[:, 0:512], SH2[:, 512:1024]]
    for i in range(16):
        hsl = 2048 * (i % 2)
        emit("scalar", lambda e, i=i, hsl=hsl: e.activation(
            hbuf[0:64, hsl:hsl + 2048], ssb[0:64, :], AF.Relu,
            bias=b1f[:, i:i + 1], scale=w1f[:, i:i + 1]),
            "S", 1, ("h", i),
            waits=[("sev", 3), E(("gmm", i - 2, 3)), ("ev", ("v", 15, 1))])
        for g4 in range(NG):
            emit("tensor", lambda e, i=i, g4=g4, hsl=hsl: e.matmul(
                gbank[g4][0:64, :], diagb[0:64, 64 * i:64 * i + 64],
                hbuf[0:64, hsl + 512 * g4:hsl + 512 * g4 + 512],
                start=(i == 0), stop=(i == 15), skip_group_check=True),
                "T", 1, ("gmm", i, g4),
                waits=[("h", i), ("v", "diag", 15)] if g4 == 0 else ())
    for g4 in range(NG):
        emit("vector", lambda e, g4=g4: e.tensor_copy(
            gbf[0:64, 512 * g4:512 * g4 + 512], gbank[g4][0:64, :]),
            "V", 1, ("gev", g4), waits=[("gmm", 15, g4)])
    for h5 in range(2):
        emit("tensor", lambda e, h5=h5: e.matmul(
            SH0[0:1, 0:512], b2b[0:64, 0:1], woutb[0:64, 512 * h5:512 * h5 + 512],
            start=True, stop=True),
            "T", 1, ("b2f", h5),
            waits=[("wl", "wout"), ("wl", "b2"), ("gev", 3), E(("bp", h5 - 1))] +
                  ([sev for sev in [slice_ev[0]] if sev is not None]))
        emit("vector", lambda e, h5=h5: e.tensor_tensor(
            boutp[0:1, 512 * h5:512 * h5 + 512], boutf[0:1, 512 * h5:512 * h5 + 512],
            SH0[0:1, 0:512], ALU.add),
            "V", 1, ("bp", h5), waits=[("b2f", h5), ("ld", "smalls")])

    if debug:
        emit("sync", lambda e: e.dma_start(dbg_xnt_early, xnT[:, :]), "DO", 16,
             ("xntdump",), waits=[("mm", ("s", 3)), ("tr", 15)])
        emit("sync", lambda e: e.dma_start(dbg_xnb, xnb[:, :]), "DO", 16, ("xnbdump",),
             waits=[("xn", 15)])
        emit("sync", lambda e: e.dma_start(dbg_x2, x2[:, :]), "DO", 16, ("x2dump",),
             waits=[("xn", 15)])
        emit("sync", lambda e: e.dma_start(dbg_stat, stat[:, :]), "DO", 16, ("statdump",),
             waits=[("xn", 15)])

    emit("vector", lambda e: e.memset(maskst[:, :], 0.0), "V", 1, ("v", "mz"),
         waits=[("mm", ("s", 3))])
    emit("gpsimd", lambda e: e.affine_select(
        i128[:, :], onesc[:, :], [[1, 128]], ALU.is_equal, 0.0,
        base=0, channel_multiplier=-1), "G", 1, waits=[("mm", ("s", 3)), ("v", "ones")])
    emit("gpsimd", lambda e: e.affine_select(
        maskst[:, 0:128], maskst[:, 0:128], [[1, 128]], ALU.is_ge, -1e30,
        base=0, channel_multiplier=-1), "G", 1, ("g", "maskinit"), waits=[("v", "mz")])
    emit("vector", lambda e: e.memset(stag[0:65, :], 1.0), "V", 1, ("v", "stagclaim"),
         waits=[("mm", ("s", 3))])

    PV = ("bp", 1)
    chunk_seq = []
    for g in range(NG):
        for p in range(NP):
            for c in range(4 * g + 4):
                chunk_seq.append((g, p, c))
    seq_of = {k: i for i, k in enumerate(chunk_seq)}

    pair_list = [(g, p) for g in range(NG) for p in range(NP)]

    def emit_pair_attention_rest(g, p):
        nch_pair = 4 * g + 4
        for c in range(2, nch_pair + 4, 2):
            if c < nch_pair:
                emit_scores(g, p, c)
                emit_scores(g, p, c + 1)
            if c >= 4:
                emit_ctx(g, p, c - 4)
                emit_ctx(g, p, c - 3)

    def emit_scores(g, p, c):
            seq = seq_of[(g, p, c)]
            j = c - 4 * g
            st = 128 * j if j > 0 else 0
            sh = SH[seq % 3]
            r = seq % RING
            diag = (j >= 0)
            w0 = [E(("exp", seq - 3))]
            if seq == 0:
                w0 += [PV, ("gev", 3), ("sev", 3), ("g", "maskinit"), E(("xntdump",))]
            emit("tensor", lambda e, sh=sh, p=p, c=c, g=g, st=st, diag=diag: e.matmul(
                sh[:, st:512], kT[0:64, 2048 * p + 128 * c:2048 * p + 128 * c + 128],
                qT[0:64, 2048 * p + 512 * g + st:2048 * p + 512 * g + 512],
                start=True, stop=not diag, tile_position=(0, 0), skip_group_check=True),
                None, waits=w0)
            emit("tensor", lambda e, sh=sh, p=p, c=c, g=g, st=st, diag=diag: e.matmul(
                sh[:, 512 + st:1024], kT[64:128, 2048 * p + 128 * c:2048 * p + 128 * c + 128],
                qT[64:128, 2048 * p + 512 * g + st:2048 * p + 512 * g + 512],
                start=True, stop=not diag, tile_position=(64, 0), skip_group_check=True),
                None if diag else "T", 1, None if diag else ("sc", seq))
            if diag:
                emit("tensor", lambda e, sh=sh, st=st: e.matmul(
                    sh[:, st:st + 128], i128[:, :], maskst[:, 0:128],
                    start=False, stop=True, skip_group_check=True), None)
                emit("tensor", lambda e, sh=sh, st=st: e.matmul(
                    sh[:, 512 + st:512 + st + 128], i128[:, :], maskst[:, 0:128],
                    start=False, stop=True, skip_group_check=True),
                    "T", 1, ("sc", seq))
            if debug and seq == 0:
                emit("vector", lambda e, sh=sh: e.tensor_copy(lnb[:, :], sh[:, 0:512]),
                     "V", 1, waits=[("sc", seq)])
                emit("vector", lambda e, sh=sh: e.tensor_copy(stag[:, 0:512], sh[:, 512:1024]),
                     "V", 1, ("shcap", seq))
                emit("sync", lambda e: e.dma_start(dbg_sh[:, 0:512], lnb[:, :]),
                     "DO", 16, waits=[("shcap", seq)])
                emit("sync", lambda e: e.dma_start(dbg_sh[:, 512:1024], stag[:, 0:512]),
                     "DO", 16, ("shdump",))
            expw = [("sc", seq), E(("ctx", seq - RING))]
            if seq == 0:
                expw.append(("ld", "cons"))
                expw.append(E(("shcap", 0)))
            if seq == RING:
                expw.append(E(("expdump",)))
            if st == 0:
                emit("scalar", lambda e, sh=sh, r=r: e.activation(
                    expT[:, 1024 * r:1024 * r + 1024], sh[:, 0:1024], AF.Exp,
                    scale=0.125),
                    "S", 1, ("exp", seq), waits=expw)
            else:
                emit("scalar", lambda e, sh=sh, r=r, st=st: e.activation(
                    expT[:, 1024 * r + st:1024 * r + 512], sh[:, st:512], AF.Exp,
                    scale=0.125), None, waits=expw)
                emit("scalar", lambda e, sh=sh, r=r, st=st: e.activation(
                    expT[:, 1024 * r + 512 + st:1024 * r + 1024], sh[:, 512 + st:1024],
                    AF.Exp, scale=0.125),
                    "S", 1, ("exp", seq))
            emit_stairs(g, p, c)
    def emit_stairs(g, p, c):
            return

    def emit_ctx(g, p, c):
            seq = seq_of[(g, p, c)]
            j = c - 4 * g
            st = 128 * j if j > 0 else 0
            r = seq % RING
            ctxw = [("exp", seq)]
            if c == 0:
                prev = pair_list.index((g, p)) - 1
                if prev >= 0:
                    ctxw.append(("cev", pair_list[prev]))
                    if prev >= 1:
                        ctxw.append(E(("norm", pair_list[prev - 1])))
                    ctxw.append(last_oev[0])
                else:
                    ctxw.append(PV)
            vb = 1536 * c + 192 * p
            emit("tensor", lambda e, vb=vb, r=r, st=st, c=c, g=g: e.matmul(
                C0[0:65, st:512], vtm[:, vb:vb + 65],
                expT[:, 1024 * r + st:1024 * r + 512],
                start=(c == 0), stop=(c == 4 * g + 3), skip_group_check=True),
                None, waits=ctxw)
            emit("tensor", lambda e, vb=vb, r=r, st=st, c=c, g=g: e.matmul(
                C1[0:128, st:512], vtm[:, vb + 64:vb + 192],
                expT[:, 1024 * r + 512 + st:1024 * r + 1024],
                start=(c == 0), stop=(c == 4 * g + 3), skip_group_check=True),
                "T", 1, ("ctx", seq))

    def emit_pair_evac(g, p):
        pb = 512 * (p % 2)
        pk = (g, p)
        idx = pair_list.index(pk)
        wv = [("ctx", seq_of[(g, p, 4 * g + 3)])]
        if idx >= 2:
            wv.append(("recip", pair_list[idx - 2]))
        if idx == 2:
            wv.append(E(("stagdump",)))
            wv.append(E(("recbdump",)))
        emit("vector", lambda e, pb=pb: e.tensor_copy(stag[64:65, pb:pb + 512], C0[64:65, 0:512]),
             "V", 1, waits=wv + ([E(("shdump",))] if pk == (0, 0) else []))
        emit("vector", lambda e, pb=pb: e.tensor_copy(stag[0:1, pb:pb + 512], C1[0:1, 0:512]),
             "V", 1, ("scp", pk))
        co = 4096 * (g % 2) + 512 * p
        wv2 = []
        if g >= 2:
            wv2.append(("bmm", (4 * (g - 2) + 3, 1)))
        if g == 2 and p == 0:
            wv2.append(E(("ctxg0dump",)))
        emit("vector", lambda e, co=co: e.tensor_copy(ctxT2[0:64, co:co + 512], C0[0:64, 0:512]),
             "V", 1, waits=wv2)
        emit("vector", lambda e, co=co: e.tensor_copy(ctxT2[64:128, co:co + 512], C1[64:128, 0:512]),
             "V", 1, ("cev", pk))

    def emit_pair_tail(g, p):
        pk = (g, p)
        idx = pair_list.index(pk)
        pb = 512 * (p % 2)
        emit("scalar", lambda e, pb=pb: e.activation(lnb[0:65, :], stag[0:65, pb:pb + 512], AF.Ln),
             "S", 1, waits=[("scp", pk), E(("shdump",))])
        wrec = [E(("bc", pair_list[idx - 2]))] if idx >= 2 else []
        emit("scalar", lambda e, pb=pb: e.activation(
            recb[0:65, pb:pb + 512], lnb[0:65, :], AF.Exp, scale=-1.0),
            "S", 1, ("recip", pk), waits=wrec)
        wb = [("recip", pk), ("cev", pk)]
        if idx + 1 < len(pair_list) and ("cev", pair_list[idx + 1]) in ev:
            wb.append(("cev", pair_list[idx + 1]))
        if idx >= 1:
            wb.append(("norm", pair_list[idx - 1]))
        else:
            wb.append(("bp", 1))
        emit("tensor", lambda e, pb=pb: e.matmul(
            BB[0:64, :], onesc[64:65, 0:64], recb[64:65, pb:pb + 512], start=True, stop=True),
            None, waits=wb)
        emit("tensor", lambda e, pb=pb: e.matmul(
            BB[64:128, :], onesc[0:1, 0:64], recb[0:1, pb:pb + 512], start=True, stop=True),
            "T", 1, ("bc", pk))
        co = 4096 * (g % 2) + 512 * p
        emit("vector", lambda e, co=co: e.tensor_tensor(
            ctxT2[:, co:co + 512], ctxT2[:, co:co + 512], BB[0:128, 0:512], ALU.mult),
            "V", 1, ("norm", pk), waits=[("bc", pk), ("cev", pk)])

    last_oev = [None]

    def emit_burst(gm1, bi):
        tc = 4 * gm1 + bi // 2
        half = bi % 2
        gbuf = gm1 % 2
        tl = tc % 4
        if gm1 == 0 and bi == 0:
            emit_x2_load(0)
            emit_x2_load(1)
        wv = [("norm", (gm1, 7)), ("wl", "wo"), last_oev[0], ("gev", 2)]
        wv = [w for w in wv if w is not None]
        for cc in range(8):
            emit("tensor", lambda e, cc=cc, gbuf=gbuf, tl=tl, half=half: e.matmul(
                WOB[0:128, :], ctxT2[:, 4096 * gbuf + 512 * cc + 128 * tl:4096 * gbuf + 512 * cc + 128 * tl + 128],
                wob[:, 1024 * cc + 512 * half:1024 * cc + 512 * half + 512],
                start=(cc == 0), stop=False, skip_group_check=True),
                None, waits=wv if cc == 0 else ())
        emit("tensor", lambda e, tc=tc, half=half: e.matmul(
            WOB[0:128, :], gbf[0:64, 128 * tc:128 * tc + 128],
            woutb[0:64, 512 * half:512 * half + 512],
            start=False, stop=False, skip_group_check=True), None)
        emit("tensor", lambda e, half=half: e.matmul(
            WOB[0:128, :], onesc[0:1, 0:128], boutp[0:1, 512 * half:512 * half + 512],
            start=False, stop=True, skip_group_check=True),
            "T", 1, ("bmm", (tc, half)))
        xoff = 1024 * (tc % 2) + 512 * half
        emit("vector", lambda e, xoff=xoff: e.tensor_tensor(
            outb[:, xoff:xoff + 512], x2[:, xoff:xoff + 512], WOB[0:128, 0:512], ALU.add),
            "V", 1, ("oev", (tc, half)),
            waits=[("bmm", (tc, half)), ("x2", tc), E(("ost", (tc - 2, half)))])
        last_oev[0] = ("oev", (tc, half))
        emit("sync", lambda e, tc=tc, half=half, xoff=xoff: e.dma_start(
            y_h[128 * tc:128 * tc + 128, 512 * half:512 * half + 512],
            outb[:, xoff:xoff + 512]),
            "DO", 16, ("ost", (tc, half)), waits=[("oev", (tc, half))])
        if half == 1 and tc + 2 <= 15:
            emit_x2_load(tc + 2)

    def emit_x2_load(tc):
        wv = [("lnxc", 15), E(("x2dump",))]
        if tc >= 2:
            wv.append(("oev", (tc - 2, 1)))
        emit("sync", lambda e, tc=tc: e.dma_start(
            x2[:, 1024 * (tc % 2):1024 * (tc % 2) + 1024], x_h[128 * tc:128 * tc + 128, :]),
            "DX", 16, ("x2", tc), waits=wv)

    all_bursts = [(gm1, b) for gm1 in range(4) for b in range(8)]
    for idx, (g, p) in enumerate(pair_list):
        emit_scores(g, p, 0)
        emit_scores(g, p, 1)
        if idx >= 2:
            emit_pair_tail(*pair_list[idx - 2])
        bk = idx - 9
        if 0 <= bk < 24:
            emit_burst(*all_bursts[bk])
        emit_pair_attention_rest(g, p)
        emit_pair_evac(g, p)
        if debug and idx == 0:
            emit("sync", lambda e: e.dma_start(dbg_exp00, expT[:, :]), "DO", 16,
                 ("expdump",), waits=[("ctx", seq_of[(0, 0, 3)])])
        if debug and idx == 1:
            emit("sync", lambda e: e.dma_start(dbg_stag00, stag[:, :]), "DO", 16,
                 ("stagdump",), waits=[("scp", (0, 0)), ("recip", (0, 0))])
            emit("sync", lambda e: e.dma_start(dbg_recb00, recb[:, :]), "DO", 16, ("recbdump",))
        if debug and idx == 8:
            emit("sync", lambda e: e.dma_start(dbg_ctxg0, ctxT2[:, 0:4096]), "DO", 16,
                 ("ctxg0dump",), waits=[("norm", (0, 7))])
    emit_pair_tail(3, 6)
    emit_pair_tail(3, 7)
    for bk in range(23, 32):
        emit_burst(*all_bursts[bk])

    if debug:
        emit("sync", lambda e: e.dma_start(dbg_kT, kT[:, :]), "DO", 16,
             waits=[last_oev[0]])
        emit("sync", lambda e: e.dma_start(dbg_qT, qT[:, :]), "DO", 16)
        emit("sync", lambda e: e.dma_start(dbg_ctx, ctxT2[:, :]), "DO", 16)
        emit("sync", lambda e: e.dma_start(dbg_vtm, vtm[:, :]), "DO", 16)
        emit("sync", lambda e: e.dma_start(dbg_xnt, xnT[:, :]), "DO", 16)

    nstores = cnt["DO"]

    with nc.Block() as block:
        @block.sync
        def _(sync):
            for f in ops["sync"]:
                f(sync)
            sync.wait_ge(semh["DO"], nstores)

        @block.tensor
        def _(tensor):
            for f in ops["tensor"]:
                f(tensor)

        @block.scalar
        def _(scalar):
            for f in ops["scalar"]:
                f(scalar)

        @block.vector
        def _(vector):
            for f in ops["vector"]:
                f(vector)

        @block.gpsimd
        def _(gpsimd):
            for f in ops["gpsimd"]:
                f(gpsimd)

    for c in ctxs:
        pass
    return nc


_NC_CACHE = None


def kernel(L0_attn_x, L0_attn_W_DKV, L0_attn_W_UK, L0_attn_W_UV, L0_attn_W_UQ,
           L0_attn_W_O, L0_ffn_Win, L0_ffn_bin_, L0_ffn_W1, L0_ffn_b1, L0_ffn_W2,
           L0_ffn_b2, L0_ffn_Wout, L0_ffn_bout):
    global _NC_CACHE
    x = np.ascontiguousarray(np.asarray(L0_attn_x, dtype=np.float32))
    consts = np.zeros((128, 8), np.float32)
    consts[:, 0] = -1.0 / 1024.0
    consts[:, 1] = 1.0 / 1024.0
    consts[:, 2] = 1e-5
    consts[:, 3] = 0.125
    consts[:, 4] = -1.0
    base = {
        "W_DKV": np.ascontiguousarray(L0_attn_W_DKV, dtype=np.float32),
        "W_UK": np.ascontiguousarray(L0_attn_W_UK, dtype=np.float32),
        "W_UV": np.ascontiguousarray(L0_attn_W_UV, dtype=np.float32),
        "W_UQ": np.ascontiguousarray(L0_attn_W_UQ, dtype=np.float32),
        "W_O": np.ascontiguousarray(L0_attn_W_O, dtype=np.float32),
        "Win": np.ascontiguousarray(L0_ffn_Win, dtype=np.float32),
        "bin_": np.ascontiguousarray(L0_ffn_bin_, dtype=np.float32),
        "W1": np.ascontiguousarray(L0_ffn_W1, dtype=np.float32),
        "b1": np.ascontiguousarray(L0_ffn_b1, dtype=np.float32),
        "W2": np.ascontiguousarray(L0_ffn_W2, dtype=np.float32),
        "b2": np.ascontiguousarray(L0_ffn_b2, dtype=np.float32),
        "Wout": np.ascontiguousarray(L0_ffn_Wout, dtype=np.float32),
        "bout": np.ascontiguousarray(L0_ffn_bout, dtype=np.float32),
        "consts": consts,
    }
    in_maps = []
    for core in range(8):
        m = dict(base)
        m["x"] = np.ascontiguousarray(x[core])
        in_maps.append(m)
    if _NC_CACHE is None:
        _NC_CACHE = build_core()
    res = run_bass_kernel_spmd(_NC_CACHE, in_maps, core_ids=list(range(8)))
    return np.stack([res.results[i]["y"] for i in range(8)], axis=0)
